# version 03
# baseline (speedup 1.0000x reference)
import numpy as np
import ml_dtypes
import concourse.bass as bass
import concourse.mybir as mybir
from concourse.bass_utils import run_bass_kernel_spmd

F32 = mybir.dt.float32
BF16 = mybir.dt.bfloat16
AF = mybir.ActivationFunctionType
ALU = mybir.AluOpType
AX = mybir.AxisListType

import os
NCORE = 8
T = int(os.environ.get("BASS_KERNEL_T", "4096"))
D = 1024
NT = T // 128
SHIFT0 = 40.0
LN_EPS = 1e-5
L0_EPS = 1e-6
RMS_EPS = 1e-5


class Sem:
    def __init__(self, nc, name):
        self.h = nc.semaphore(name).__enter__()
        self.n = 0

    def inc(self, k=1):
        self.n += k
        return self.n


class B:
    def __init__(self):
        self.nc = bass.Bass("TRN2", target_bir_lowering=False, debug=False)

    def sem(self, name):
        return Sem(self.nc, name)

    def sb(self, name, shape, dtype):
        return self.nc.sbuf_tensor(name, shape, dtype).__enter__()

    def ps(self, name, shape, dtype=F32):
        return self.nc.psum_tensor(name, shape, dtype).__enter__()

    def din(self, name, shape, dtype):
        return self.nc.dram_tensor(name, shape, dtype, kind="ExternalInput")

    def dout(self, name, shape, dtype):
        return self.nc.dram_tensor(name, shape, dtype, kind="ExternalOutput")

    def dtmp(self, name, shape, dtype):
        return self.nc.dram_tensor(name, shape, dtype, kind="Internal")


def emit_ln_vector(vector, sc, vv, x_ap, out_ap, s_sq, s_st, sc0):
    vector.tensor_reduce(s_st[:, 0:1], x_ap, axis=AX.X, op=ALU.add)
    vector.wait_ge(sc.h, sc0 + 1)
    vector.tensor_reduce(s_st[:, 1:2], s_sq[:], axis=AX.X, op=ALU.add)
    vector.drain()
    vector.tensor_scalar(s_st[:, 2:3], s_st[:, 0:1], 1.0 / D, None, op0=ALU.mult)
    vector.tensor_scalar(s_st[:, 3:4], s_st[:, 1:2], 1.0 / D, None, op0=ALU.mult)
    vector.drain()
    vector.tensor_tensor(s_st[:, 4:5], s_st[:, 2:3], s_st[:, 2:3], op=ALU.mult)
    vector.drain()
    vector.tensor_tensor(s_st[:, 5:6], s_st[:, 3:4], s_st[:, 4:5], op=ALU.subtract)
    vector.drain()
    vector.tensor_scalar(s_st[:, 5:6], s_st[:, 5:6], LN_EPS, None,
                         op0=ALU.add).then_inc(vv.h, 1); vv.inc()
    vector.wait_ge(sc.h, sc0 + 2)
    vector.reciprocal(s_st[:, 7:8], s_st[:, 6:7])
    vector.drain()
    return vector.scalar_tensor_tensor(out_ap, x_ap, s_st[:, 2:3],
                                       s_st[:, 7:8].broadcast_to([128, D]),
                                       op0=ALU.subtract, op1=ALU.mult)


def emit_ln_scalar(scalar, sc, vv, x_ap, s_sq, s_st, vv0):
    scalar.square(s_sq[:], x_ap).then_inc(sc.h, 1); sc.inc()
    scalar.wait_ge(vv.h, vv0 + 1)
    scalar.activation(s_st[:, 6:7], s_st[:, 5:6],
                      AF.Sqrt).then_inc(sc.h, 1); sc.inc()


def build_launch1():
    b = B()
    nc = b.nc
    x = b.din("x", [T, D], F32).ap()
    wq_h = b.din("wq_h", [16, 64, D], F32).ap()
    wk_h = b.din("wk_h", [16, 64, D], F32).ap()
    w0 = b.din("w0", [16, 64, 64], F32).ap()
    bq2 = b.din("bq2", [64, 16], F32).ap()
    bk2 = b.din("bk2", [64, 16], F32).ap()
    wvT = b.din("wvT", [D, D], F32).ap()
    wkT = b.din("wkT", [D, D], F32).ap()
    bv_row = b.din("bv_row", [1, D], F32).ap()
    bk_row = b.din("bk_row", [1, D], F32).ap()
    ones_row = b.din("ones_row", [1, 128], F32).ap()
    ones_col = b.din("ones_col", [128, 1], F32).ap()

    xn_bf = b.dout("xn_bf", [T, D], BF16).ap()
    phi_bf = b.dout("phi_bf", [T, D], BF16).ap()
    stats_N = b.dout("stats_N", [16, 64, 64], F32).ap()
    stats_D = b.dout("stats_D", [16, 64], F32).ap()

    DBG = os.environ.get("DEV_DBG") == "1"
    if DBG:
        s_logits = b.dout("s_logits", [T, D], F32).ap()
        s_v = b.dout("s_v", [T, D], BF16).ap()
        dbg_wqeff = b.dout("dbg_wqeff", [128, 8 * D], BF16).ap()
        dbg_beffq = b.dout("dbg_beffq", [1, D], BF16).ap()
        dbg_xnfm = b.dout("dbg_xnfm", [128, 8 * 512], BF16).ap()
    else:
        s_logits = b.dtmp("s_logits", [T, D], F32).ap()
        s_v = b.dtmp("s_v", [T, D], BF16).ap()
    s_beff = b.dtmp("s_beff", [2, 16, 64], F32).ap()

    s_wvT = b.sb("s_wvT", [128, 8, D], BF16)
    s_wkT = b.sb("s_wkT", [128, 8, D], BF16)
    s_whst = [b.sb("s_whst0", [64, 8, 128], BF16), b.sb("s_whst1", [64, 8, 128], BF16)]
    s_w0 = b.sb("s_w0", [64, 16, 64], BF16)
    s_bq2 = b.sb("s_bq2", [64, 16], BF16)
    s_bk2 = b.sb("s_bk2", [64, 16], BF16)
    s_bvrow = b.sb("s_bvrow", [1, D], BF16)
    s_bkrow = b.sb("s_bkrow", [1, D], BF16)
    s_ones = b.sb("s_ones", [1, 128], BF16)
    s_ones128 = b.sb("s_ones128", [128, 1], BF16)
    s_wqeff = b.sb("s_wqeff", [128, 8, D], BF16)
    s_wkeff = b.sb("s_wkeff", [128, 8, D], BF16)
    s_beffq = b.sb("s_beffq", [1, D], BF16)
    s_beffk = b.sb("s_beffk", [1, D], BF16)
    s_beff64 = b.sb("s_beff64", [64, 16], F32)

    s_x = b.sb("s_x", [128, D], F32)
    s_sq = b.sb("s_sq", [128, D], F32)
    s_st = b.sb("s_st", [128, 8], F32)
    s_xnbf = b.sb("s_xnbf", [128, D], BF16)
    s_xG = b.sb("s_xG", [128, 4, D], F32)
    s_sqG = b.sb("s_sqG", [128, 4, D], F32)
    s_xnG = b.sb("s_xnG", [128, 4, D], BF16)
    s_stG = b.sb("s_stG", [128, 4, 4], F32)

    s_xnfm = b.sb("s_xnfm", [128, 8, 512], BF16)
    s_ss = b.sb("s_ss", [128, 16], F32)
    s_lt = b.sb("s_lt", [128, D], F32)
    s_pt = b.sb("s_pt", [128, D], F32)
    s_pe = b.sb("s_pe", [128, D], F32)
    s_ph = b.sb("s_ph", [128, 16], F32)
    s_ph2 = b.sb("s_ph2", [128, 16], F32)
    s_vt = b.sb("s_vt", [128, D], BF16)
    s_phbf = b.sb("s_phbf", [128, D], BF16)
    s_at = b.sb("s_at", [128, D], BF16)
    s_lg2 = b.sb("s_lg2", [128, D], F32)
    s_vt2 = b.sb("s_vt2", [128, D], BF16)
    s_Nout = b.sb("s_Nout", [64, 1024], F32)
    s_Dout = b.sb("s_Dout", [64, 16], F32)

    ps_a = b.ps("ps_a", [128, 512])
    ps_b = b.ps("ps_b", [128, 512])
    ps_c = b.ps("ps_c", [128, 512])
    ps_d = b.ps("ps_d", [128, 512])
    ps_N1 = b.ps("ps_N1", [64, 512])
    ps_N2 = b.ps("ps_N2", [64, 512])
    ps_D = b.ps("ps_D", [64, 16])

    ld = b.sem("ld")
    lg = b.sem("lg")
    ssy = b.sem("ssy")
    sg = b.sem("sg")
    tv = b.sem("tv")
    vv = b.sem("vv")
    sc = b.sem("sc")

    with nc.Block() as k:
        @k.gpsimd
        def _(g):
            for dst, src in ((s_w0, w0.transpose([1, 0, 2])), (s_bq2, bq2),
                             (s_bk2, bk2), (s_bvrow, bv_row), (s_bkrow, bk_row),
                             (s_ones, ones_row), (s_ones128, ones_col)):
                g.dma_start(dst[:], src).then_inc(lg.h, 16); lg.inc(16)
            g.dma_start(s_wvT[:], wvT.rearrange("(c p) j -> p c j", p=128)).then_inc(lg.h, 16); lg.inc(16)
            g.dma_start(s_wkT[:], wkT.rearrange("(c p) j -> p c j", p=128)).then_inc(lg.h, 16); lg.inc(16)
            g.wait_ge(lg.h, lg.n)

    banks = [ps_a, ps_b, ps_c, ps_d]
    for wi in range(2):
        srcd = wq_h if wi == 0 else wk_h
        dst = s_wqeff if wi == 0 else s_wkeff
        for h in range(16):
            with nc.Block() as k:
                lg_tot = lg.n + 16

                @k.gpsimd
                def _(g, wi=wi, h=h, srcd=srcd, lg_tot=lg_tot):
                    g.dma_start(s_whst[h % 2][:],
                                srcd[h].rearrange("s (c p) -> s c p", p=128)).then_inc(lg.h, 16)
                    g.wait_ge(lg.h, lg_tot)
                lg.inc(16)

                @k.tensor
                def _(tensor, wi=wi, h=h, lg_tot=lg_tot):
                    tensor.wait_ge(lg.h, lg_tot)
                    pb = banks[h % 4]
                    for c in range(8):
                        tensor.matmul(pb[:, c * 64:(c + 1) * 64],
                                      s_whst[h % 2][:, c, :], s_w0[:, h, :],
                                      start=(c == 0), stop=(c == 7),
                                      skip_group_check=True)
            with nc.Block() as k:
                @k.vector
                def _(vector, wi=wi, h=h, dst=dst):
                    pb = banks[h % 4]
                    for c in range(8):
                        vector.tensor_copy(dst[:, c, h * 64:(h + 1) * 64],
                                           pb[:, c * 64:(c + 1) * 64])
    for wi in range(2):
        bcol = s_bq2 if wi == 0 else s_bk2
        with nc.Block() as k:
            @k.tensor
            def _(tensor, bcol=bcol):
                for h in range(16):
                    tensor.matmul(ps_D[:, h:h + 1], s_w0[:, h, :], bcol[:, h:h + 1],
                                  start=(h == 0), stop=(h == 15),
                                  skip_group_check=True)
        with nc.Block() as k:
            @k.vector
            def _(vector):
                vector.tensor_copy(s_beff64[:], ps_D[:])
        with nc.Block() as k:
            @k.sync
            def _(sync, wi=wi):
                sync.dma_start(s_beff[wi].transpose([1, 0]), s_beff64[:],
                               allow_slow_non_contiguous=True,
                               ).then_inc(ssy.h, 16); ssy.inc(16)
                sync.wait_ge(ssy.h, ssy.n)
    with nc.Block() as k:
        @k.gpsimd
        def _(g):
            g.dma_start(s_beffq[:], s_beff[0].rearrange("h f -> (h f)").unsqueeze(0)).then_inc(lg.h, 16); lg.inc(16)
            g.dma_start(s_beffk[:], s_beff[1].rearrange("h f -> (h f)").unsqueeze(0)).then_inc(lg.h, 16); lg.inc(16)
            g.wait_ge(lg.h, lg.n)

    for gi in range(T // 512):
        with nc.Block() as k:
            ld_tot = ld.n + 16
            sc0 = sc.n
            sg_before = sg.n

            @k.sync
            def _(sync, gi=gi, ld_tot=ld_tot):
                sync.dma_start(s_xG[:], x[gi * 512:(gi + 1) * 512, :]
                               .rearrange("(t p) d -> p t d", p=128)).then_inc(ld.h, 16)
                sync.wait_ge(ld.h, ld_tot)
            ld.inc(16)

            @k.vector
            def _(vector, ld_tot=ld_tot, sc0=sc0, sg_before=sg_before):
                vector.wait_ge(ld.h, ld_tot)
                vector.wait_ge(sg.h, sg_before)
                vector.tensor_reduce(s_stG[:, 0, :].unsqueeze(2), s_xG.ap(),
                                     axis=AX.X, op=ALU.add)
                vector.wait_ge(sc.h, sc0 + 1)
                vector.tensor_reduce(s_stG[:, 1, :].unsqueeze(2), s_sqG.ap(),
                                     axis=AX.X, op=ALU.add)
                vector.drain()
                vector.tensor_scalar(s_stG[:, 0:2, :], s_stG[:, 0:2, :], 1.0 / D,
                                     None, op0=ALU.mult)
                vector.drain()
                vector.tensor_tensor(s_stG[:, 2, :], s_stG[:, 0, :], s_stG[:, 0, :],
                                     op=ALU.mult)
                vector.drain()
                vector.tensor_tensor(s_stG[:, 2, :], s_stG[:, 1, :], s_stG[:, 2, :],
                                     op=ALU.subtract)
                vector.drain()
                vector.tensor_scalar(s_stG[:, 2, :], s_stG[:, 2, :], LN_EPS, None,
                                     op0=ALU.add).then_inc(vv.h, 1); vv.inc()
                vector.wait_ge(sc.h, sc0 + 2)
                vector.reciprocal(s_stG[:, 3, :], s_stG[:, 2, :])
                vector.drain()
                vector.tensor_tensor(s_sqG.ap(), s_xG.ap(),
                                     s_stG[:, 0, :].unsqueeze(2).broadcast_to([128, 4, D]),
                                     op=ALU.subtract)
                vector.drain()
                vector.tensor_tensor(s_xnG.ap(), s_sqG.ap(),
                                     s_stG[:, 3, :].unsqueeze(2).broadcast_to([128, 4, D]),
                                     op=ALU.mult).then_inc(vv.h, 1); vv.inc()

            @k.scalar
            def _(scalar, ld_tot=ld_tot):
                scalar.wait_ge(ld.h, ld_tot)
                scalar.square(s_sqG.ap().rearrange("p t d -> p (t d)"),
                              s_xG.ap().rearrange("p t d -> p (t d)")).then_inc(sc.h, 1); sc.inc()
                scalar.wait_ge(vv.h, vv.n - 1)
                scalar.activation(s_stG[:, 2, :], s_stG[:, 2, :],
                                  AF.Sqrt).then_inc(sc.h, 1); sc.inc()

            @k.gpsimd
            def _(g, gi=gi):
                g.wait_ge(vv.h, vv.n)
                g.dma_start(xn_bf[gi * 512:(gi + 1) * 512, :]
                            .rearrange("(t p) d -> p t d", p=128),
                            s_xnG[:]).then_inc(sg.h, 16)
            sg.inc(16)
    with nc.Block() as k:
        @k.gpsimd
        def _(g):
            g.wait_ge(sg.h, sg.n)

    NG = T // 512
    def proj_tile(tensor, xg, t, rhs, bias_row, pdst0, pdst1, first_starts):
        for jh, pdst in ((0, pdst0), (1, pdst1)):
            for kc in range(8):
                tensor.matmul(pdst, s_xnfm[:, kc, t * 128:(t + 1) * 128],
                              rhs[:, kc, jh * 512:(jh + 1) * 512],
                              start=(kc == 0), stop=False, skip_group_check=True)
            tensor.matmul(pdst, s_ones[:, 0:128], bias_row[:, jh * 512:(jh + 1) * 512],
                          start=False, stop=True, skip_group_check=True)

    for g in range(NG):
        with nc.Block() as k:
            @k.sync
            def _(sync, g=g):
                for kc in range(8):
                    sync.dma_start_transpose(
                        s_xnfm[:, kc, :],
                        xn_bf[g * 512:(g + 1) * 512, kc * 128:(kc + 1) * 128],
                    ).then_inc(ld.h, 16); ld.inc(16)
                sync.wait_ge(ld.h, ld.n)
        for t in range(4):
            i = g * 4 + t
            with nc.Block() as k:
                @k.tensor
                def _(tensor, g=g, t=t):
                    proj_tile(tensor, g, t, s_wvT, s_bvrow, ps_a[:], ps_b[:], True)
                    proj_tile(tensor, g, t, s_wkT, s_bkrow, ps_c[:], ps_d[:], True)
            with nc.Block() as k:
                @k.vector
                def _(vector):
                    vector.tensor_copy(s_vt[:, 0:512], ps_a[:])
                    vector.tensor_copy(s_vt[:, 512:1024], ps_b[:])

                @k.scalar
                def _(scalar):
                    scalar.square(s_sq[:, 0:512], ps_c[:])
                    scalar.square(s_sq[:, 512:1024], ps_d[:])
            with nc.Block() as k:
                @k.vector
                def _(vector):
                    vector.tensor_reduce(
                        s_ss.ap().rearrange("p (h o) -> p h o", o=1),
                        s_sq.ap().rearrange("p (h f) -> p h f", h=16),
                        axis=AX.X, op=ALU.add)
                    vector.drain()
                    vector.tensor_scalar(s_ss[:], s_ss[:], 0.5, None, op0=ALU.mult)
            with nc.Block() as k:
                @k.tensor
                def _(tensor, g=g, t=t):
                    proj_tile(tensor, g, t, s_wkeff, s_beffk, ps_a[:], ps_b[:], True)
                    proj_tile(tensor, g, t, s_wqeff, s_beffq, ps_c[:], ps_d[:], True)
            with nc.Block() as k:
                @k.vector
                def _(vector):
                    for jh, pb in ((0, ps_a), (1, ps_b)):
                        vector.scalar_tensor_tensor(
                            s_lt.ap()[:, jh * 512:(jh + 1) * 512].rearrange("p (h f) -> p h f", h=8),
                            s_ss.ap()[:, jh * 8:(jh + 1) * 8].unsqueeze(2).broadcast_to([128, 8, 64]),
                            -1.0,
                            pb.ap().rearrange("p (h f) -> p h f", h=8),
                            op0=ALU.mult, op1=ALU.add)
                    for jh, pb in ((0, ps_c), (1, ps_d)):
                        vector.tensor_reduce(
                            s_ph.ap()[:, jh * 8:(jh + 1) * 8].unsqueeze(2),
                            pb.ap().rearrange("p (h f) -> p h f", h=8),
                            axis=AX.X, op=ALU.max, negate=True)
                    vector.drain()
                    vector.tensor_tensor(
                        s_pt.ap()[:, 0:512].rearrange("p (h f) -> p h f", h=8),
                        ps_c.ap().rearrange("p (h f) -> p h f", h=8),
                        s_ph.ap()[:, 0:8].unsqueeze(2).broadcast_to([128, 8, 64]),
                        op=ALU.add)
                    vector.tensor_tensor(
                        s_pt.ap()[:, 512:1024].rearrange("p (h f) -> p h f", h=8),
                        ps_d.ap().rearrange("p (h f) -> p h f", h=8),
                        s_ph.ap()[:, 8:16].unsqueeze(2).broadcast_to([128, 8, 64]),
                        op=ALU.add).then_inc(vv.h, 1); vv.inc()

                @k.scalar
                def _(scalar):
                    scalar.wait_ge(vv.h, vv.n)
                    scalar.activation(s_pe[:], s_pt[:], AF.Exp).then_inc(sc.h, 1); sc.inc()

                @k.vector
                def _(vector):
                    vector.wait_ge(sc.h, sc.n)
                    vector.tensor_reduce(
                        s_ph.ap().rearrange("p (h o) -> p h o", o=1),
                        s_pe.ap().rearrange("p (h f) -> p h f", h=16),
                        axis=AX.X, op=ALU.add)
                    vector.drain()
                    vector.reciprocal(s_ph2[:], s_ph[:])
                    vector.drain()
                    vector.tensor_tensor(
                        s_phbf.ap().rearrange("p (h f) -> p h f", h=16),
                        s_pe.ap().rearrange("p (h f) -> p h f", h=16),
                        s_ph2.ap().unsqueeze(2).broadcast_to([128, 16, 64]),
                        op=ALU.mult)
            with nc.Block() as k:
                @k.gpsimd
                def _(g_, i=i):
                    g_.dma_start(s_v[i * 128:(i + 1) * 128, :], s_vt[:]).then_inc(sg.h, 16); sg.inc(16)
                    g_.dma_start(s_logits[i * 128:(i + 1) * 128, :], s_lt[:]).then_inc(sg.h, 16); sg.inc(16)
                    g_.dma_start(phi_bf[i * 128:(i + 1) * 128, :], s_phbf[:]).then_inc(sg.h, 16); sg.inc(16)
                    g_.wait_ge(sg.h, sg.n)

    for i in range(NT):
        with nc.Block() as k:
            ld_tot = ld.n + 32

            @k.sync
            def _(sync, i=i, ld_tot=ld_tot):
                sync.dma_start(s_lg2[:], s_logits[i * 128:(i + 1) * 128, :]).then_inc(ld.h, 16)
                sync.dma_start(s_vt2[:], s_v[i * 128:(i + 1) * 128, :]).then_inc(ld.h, 16)
                sync.wait_ge(ld.h, ld_tot)
            ld.inc(32)

            @k.vector
            def _(vector, ld_tot=ld_tot):
                vector.wait_ge(ld.h, ld_tot)
                vector.tensor_scalar(s_lg2[:], s_lg2[:], -SHIFT0, None,
                                     op0=ALU.add).then_inc(vv.h, 1); vv.inc()

            @k.scalar
            def _(scalar):
                scalar.wait_ge(vv.h, vv.n)
                scalar.activation(s_at[:], s_lg2[:], AF.Exp).then_inc(sc.h, 1); sc.inc()

            @k.tensor
            def _(tensor, i=i):
                tensor.wait_ge(sc.h, sc.n)
                for h in range(16):
                    pb = ps_N1 if h < 8 else ps_N2
                    cc = (h % 8) * 64
                    tensor.matmul(pb[:, cc:cc + 64],
                                  s_at[:, h * 64:(h + 1) * 64],
                                  s_vt2[:, h * 64:(h + 1) * 64],
                                  start=(i == 0 and h % 8 == 0), stop=(i == NT - 1),
                                  skip_group_check=True)
                for h in range(16):
                    tensor.matmul(ps_D[:, h:h + 1],
                                  s_at[:, h * 64:(h + 1) * 64],
                                  s_ones128[:],
                                  start=(i == 0 and h == 0), stop=(i == NT - 1),
                                  skip_group_check=True)
    with nc.Block() as k:
        @k.vector
        def _(vector):
            vector.tensor_copy(s_Nout[:, 0:512], ps_N1[:])
            vector.tensor_copy(s_Nout[:, 512:1024], ps_N2[:])
            vector.tensor_copy(s_Dout[:], ps_D[:])
    if DBG:
        with nc.Block() as k:
            @k.sync
            def _(sync):
                sync.dma_start(dbg_wqeff, s_wqeff.ap().rearrange("p c j -> p (c j)")).then_inc(ssy.h, 16); ssy.inc(16)
                sync.dma_start(dbg_beffq, s_beffq[:]).then_inc(ssy.h, 16); ssy.inc(16)
                sync.dma_start(dbg_xnfm, s_xnfm.ap().rearrange("p c j -> p (c j)")).then_inc(ssy.h, 16); ssy.inc(16)
                sync.wait_ge(ssy.h, ssy.n)
    with nc.Block() as k:
        @k.sync
        def _(sync):
            sync.dma_start(stats_N.transpose([1, 0, 2]),
                           s_Nout.ap().rearrange("f (h d) -> f h d", h=16)).then_inc(ssy.h, 16); ssy.inc(16)
            with nc.allow_non_contiguous_dma(reason="tiny 1k-elem transpose store"):
                sync.dma_start(stats_D.transpose([1, 0]), s_Dout[:]).then_inc(ssy.h, 16); ssy.inc(16)
            sync.wait_ge(ssy.h, ssy.n)
    return b


def build_launch2():
    PH = int(os.environ.get("BASS_L2_PHASES", "9"))
    b = B()
    nc = b.nc
    NGR = T // 512
    x = b.din("x", [T, D], F32).ap()
    xn_bf = b.din("xn_bf", [T, D], BF16).ap()
    phi_bf = b.din("phi_bf", [T, D], BF16).ap()
    statsN4 = b.din("statsN4", [4, 16, 64, 64], F32).ap()
    statsD4 = b.din("statsD4", [4, 16, 64], F32).ap()
    woT = b.din("woT", [D, D], F32).ap()
    bo_row = b.din("bo_row", [1, D], F32).ap()
    centT = b.din("centT", [D, 16], F32).ap()
    eb_row = b.din("eb_row", [1, 16], F32).ap()
    wrT = b.din("wrT", [16, D, D], F32).ap()
    br_e = b.din("br_e", [16, D], F32).ap()
    wsT = b.din("wsT", [2, D, D], F32).ap()
    bs2 = b.din("bs2", [1, 2 * D], F32).ap()
    selp = b.din("selp", [8, 16, 128], F32).ap()
    ones_row = b.din("ones_row", [1, 128], F32).ap()
    ones_col = b.din("ones_col", [128, 1], F32).ap()
    ident = b.din("ident", [128, 128], F32).ap()

    x1 = b.dout("x1", [T, D], F32).ap()
    xn1_bf = b.dout("xn1_bf", [T, D], BF16).ap()

    dN = b.dtmp("dN", [16, 64, 64], F32).ap()
    dD = b.dtmp("dD", [16, 64], F32).ap()
    dSeps = b.dtmp("dSeps", [1, 16], F32).ap()

    s_woT = b.sb("s_woT", [128, 8, D], BF16)
    s_wsT2 = b.sb("s_wsT2", [128, 16, D], BF16)
    s_wr = [b.sb("s_wr0", [128, 8, D], BF16), b.sb("s_wr1", [128, 8, D], BF16)]
    s_centT = b.sb("s_centT", [128, 8, 16], BF16)
    s_selp = b.sb("s_selp", [16, 8, 128], BF16)
    s_ident = b.sb("s_ident", [128, 128], F32)
    s_ones = b.sb("s_ones", [1, 128], BF16)
    s_onescol = b.sb("s_onescol", [128, 1], BF16)
    s_borow = b.sb("s_borow", [1, D], BF16)
    s_ebrow = b.sb("s_ebrow", [1, 16], BF16)
    s_bre = [b.sb("s_bre0", [1, D], BF16), b.sb("s_bre1", [1, D], BF16)]
    s_bs2 = b.sb("s_bs2", [1, 2 * D], F32)
    s_bsm = b.sb("s_bsm", [1, D], BF16)
    s_Nld = b.sb("s_Nld", [64, 1024], F32)
    s_Dld = b.sb("s_Dld", [64, 16], F32)
    s_Ncmb = b.sb("s_Ncmb", [64, 1024], F32)
    s_Dcmb = b.sb("s_Dcmb", [64, 16], F32)
    s_Dcmb_bf = b.sb("s_Dcmb_bf", [64, 16], BF16)
    s_seps = b.sb("s_seps", [1, 16], F32)
    s_Npair = b.sb("s_Npair", [128, 8, 128], BF16)
    s_Dbig = b.sb("s_Dbig", [128, D], F32)
    s_SepsB = b.sb("s_SepsB", [128, 16], F32)

    s_xnfm = b.sb("s_xnfm2", [128, 8, 512], BF16)
    s_phifm = b.sb("s_phifm", [128, 8, 512], BF16)
    s_xt = b.sb("s_xt", [128, D], F32)
    s_xnt = b.sb("s_xnt", [128, D], BF16)
    s_phit = b.sb("s_phit", [128, D], BF16)
    s_attn = b.sb("s_attn", [128, 4, D], F32)
    s_acc = b.sb("s_acc", [128, 4, D], F32)
    s_wcol = b.sb("s_wcol", [128, 4, 16], F32)

    s_p32 = b.sb("s_p32", [128, D], F32)
    s_den = b.sb("s_den", [128, 16], F32)
    s_rden = b.sb("s_rden", [128, 16], F32)
    s_rdT = b.sb("s_rdT", [16, 128], F32)
    s_rdT_bf = b.sb("s_rdT_bf", [16, 128], BF16)
    s_dex = b.sb("s_dex", [128, 128], F32)
    s_ctx = b.sb("s_ctx", [128, 8, 128], BF16)
    s_lg = b.sb("s_lg", [128, 16], F32)
    s_m8 = b.sb("s_m8", [128, 8], F32)
    s_m1n = b.sb("s_m1n", [128, 1], F32)
    s_er = b.sb("s_er", [128, 16], F32)
    s_sum = b.sb("s_sum", [128, 1], F32)
    s_rs = b.sb("s_rs", [128, 1], F32)
    s_msk = b.sb("s_msk", [128, 16], F32)
    s_x1t = b.sb("s_x1t", [128, D], F32)
    s_sq = b.sb("s_sq2", [128, D], F32)
    s_st = b.sb("s_st2", [128, 8], F32)
    s_xn1t = b.sb("s_xn1t", [128, D], BF16)

    ps_a = b.ps("ps2_a", [128, 512])
    ps_b = b.ps("ps2_b", [128, 512])
    ps_e = b.ps("ps2_e", [128, 128])
    ps_n = b.ps("ps2_n", [128, 128])
    ps_t = b.ps("ps2_t", [16, 128])
    ps_r = b.ps("ps2_r", [128, 16])
    ps_s = b.ps("ps2_s", [1, 16])

    ld = b.sem("ld")
    lg = b.sem("lg")
    lw = b.sem("lw")
    ssy = b.sem("ssy")
    sg = b.sem("sg")
    tv = b.sem("tv")
    vv = b.sem("vv")
    sc = b.sem("sc")

    with nc.Block() as k:
        @k.sync
        def _(sync):
            sync.dma_start(s_ident[:], ident).then_inc(ld.h, 16); ld.inc(16)
            sync.dma_start(s_bs2[:], bs2).then_inc(ld.h, 16); ld.inc(16)
            sync.wait_ge(ld.h, ld.n)

        @k.gpsimd
        def _(g):
            g.dma_start(s_woT[:], woT.rearrange("(c p) j -> p c j", p=128)).then_inc(lg.h, 16); lg.inc(16)
            g.dma_start(s_centT[:], centT.rearrange("(c p) j -> p c j", p=128)).then_inc(lg.h, 16); lg.inc(16)
            g.dma_start(s_selp[:], selp.transpose([1, 0, 2])).then_inc(lg.h, 16); lg.inc(16)
            g.dma_start(s_ones[:], ones_row).then_inc(lg.h, 16); lg.inc(16)
            g.dma_start(s_onescol[:], ones_col).then_inc(lg.h, 16); lg.inc(16)
            g.dma_start(s_borow[:], bo_row).then_inc(lg.h, 16); lg.inc(16)
            g.dma_start(s_ebrow[:], eb_row).then_inc(lg.h, 16); lg.inc(16)
            g.dma_start(s_wsT2[:, 0:8, :], wsT[0].rearrange("(c p) j -> p c j", p=128)).then_inc(lg.h, 16); lg.inc(16)
            g.dma_start(s_wsT2[:, 8:16, :], wsT[1].rearrange("(c p) j -> p c j", p=128)).then_inc(lg.h, 16); lg.inc(16)
            g.wait_ge(lg.h, lg.n)

    for ci in range(4):
        with nc.Block() as k:
            @k.sync
            def _(sync, ci=ci):
                sync.dma_start(s_Nld.ap().rearrange("f (h d) -> f h d", h=16), statsN4[ci].transpose([1, 0, 2])).then_inc(ld.h, 16); ld.inc(16)
                with nc.allow_non_contiguous_dma(reason="tiny"):
                    sync.dma_start(s_Dld[:], statsD4[ci].transpose([1, 0])).then_inc(ld.h, 16); ld.inc(16)
                sync.wait_ge(ld.h, ld.n)
        with nc.Block() as k:
            @k.vector
            def _(vector, ci=ci):
                if ci == 0:
                    vector.tensor_copy(s_Ncmb[:], s_Nld[:])
                    vector.tensor_copy(s_Dcmb[:], s_Dld[:])
                else:
                    vector.tensor_tensor(s_Ncmb[:], s_Ncmb[:], s_Nld[:], op=ALU.add)
                    vector.tensor_tensor(s_Dcmb[:], s_Dcmb[:], s_Dld[:], op=ALU.add)
    with nc.Block() as k:
        @k.vector
        def _(vector):
            vector.tensor_copy(s_Dcmb_bf[:], s_Dcmb[:])
            vector.tensor_tensor(s_bsm[:], s_bs2[:, 0:D], s_bs2[:, D:2 * D], op=ALU.add)
    with nc.Block() as k:
        @k.tensor
        def _(tensor):
            tensor.matmul(ps_s[:], s_onescol[0:64, 0:1], s_Dcmb_bf[:],
                          start=True, stop=True, skip_group_check=True)
    with nc.Block() as k:
        @k.vector
        def _(vector):
            vector.tensor_scalar(s_seps[:], ps_s[:], L0_EPS, None, op0=ALU.mult)
    with nc.Block() as k:
        @k.sync
        def _(sync):
            sync.dma_start(dN.transpose([1, 0, 2]),
                           s_Ncmb.ap().rearrange("f (h d) -> f h d", h=16)).then_inc(ssy.h, 16); ssy.inc(16)
            with nc.allow_non_contiguous_dma(reason="tiny"):
                sync.dma_start(dD.transpose([1, 0]), s_Dcmb[:]).then_inc(ssy.h, 16); ssy.inc(16)
            sync.dma_start(dSeps, s_seps[:]).then_inc(ssy.h, 16); ssy.inc(16)
            sync.wait_ge(ssy.h, ssy.n)

        @k.vector
        def _(vector):
            vector.memset(s_Npair[:], 0.0)
    with nc.Block() as k:
        @k.sync
        def _(sync):
            sync.dma_start(s_Dbig[:], dD.rearrange("h f -> (h f)").unsqueeze(0)
                           .broadcast_to([128, D])).then_inc(ld.h, 16); ld.inc(16)
            sync.dma_start(s_SepsB[:], dSeps.broadcast_to([128, 16])).then_inc(ld.h, 16); ld.inc(16)
            sync.wait_ge(ld.h, ld.n)

        @k.gpsimd
        def _(g):
            for p in range(8):
                g.dma_start(s_Npair[0:64, p, 0:64], dN[2 * p]).then_inc(lg.h, 16); lg.inc(16)
                g.dma_start(s_Npair[64:128, p, 64:128], dN[2 * p + 1]).then_inc(lg.h, 16); lg.inc(16)
            g.wait_ge(lg.h, lg.n)

    wr_need = [0]
    wr_ready = [0, 0]
    sg_guard = [0]
    moe_tb = [0]
    moe_vb = [0]

    def moe_mm(tensor, ew, e):
        moe_tb[0] = tv.n
        vb = vv.n
        nkc = ew.shape[1]
        for t in range(4):
            if t > 0:
                tensor.wait_ge(vv.h, vb + t)
            for jh in range(2):
                pd = ps_a if jh == 0 else ps_b
                for kc in range(nkc):
                    tensor.matmul(pd[:], s_xnfm[:, kc % 8, t * 128:(t + 1) * 128],
                                  ew[:, kc, jh * 512:(jh + 1) * 512],
                                  start=(kc == 0), stop=False, skip_group_check=True)
                if e is not None:
                    brow = s_bre[e % 2][:, jh * 512:(jh + 1) * 512]
                else:
                    brow = s_bsm[:, jh * 512:(jh + 1) * 512]
                mm = tensor.matmul(pd[:], s_ones[:, 0:128], brow,
                                   start=False, stop=True, skip_group_check=True)
                if jh == 1:
                    mm.then_inc(tv.h, 1); tv.inc()

    def moe_acc(vector, e):
        for t in range(4):
            vector.wait_ge(tv.h, moe_tb[0] + t + 1)
            for jh in range(2):
                pd = ps_a if jh == 0 else ps_b
                dst = s_acc[:, t, jh * 512:(jh + 1) * 512]
                if e is not None:
                    op = vector.scalar_tensor_tensor(dst, pd[:], s_wcol[:, t, e:e + 1],
                                                     dst, op0=ALU.mult, op1=ALU.add)
                else:
                    op = vector.scalar_tensor_tensor(dst, pd[:], 0.5, dst,
                                                     op0=ALU.mult, op1=ALU.add)
                vector.drain()
                if jh == 1:
                    op.then_inc(vv.h, 1); vv.inc()

    for g in range(NGR):
        r0 = g * 512
        with nc.Block() as k:
            @k.sync
            def _(sync, g=g, r0=r0):
                for kc in range(8):
                    sync.dma_start_transpose(
                        s_xnfm[:, kc, :], xn_bf[r0:r0 + 512, kc * 128:(kc + 1) * 128]
                    ).then_inc(ld.h, 16); ld.inc(16)
                    sync.dma_start_transpose(
                        s_phifm[:, kc, :], phi_bf[r0:r0 + 512, kc * 128:(kc + 1) * 128]
                    ).then_inc(ld.h, 16); ld.inc(16)
                sync.wait_ge(ld.h, ld.n)

            if g == 0:
                @k.gpsimd
                def _(gp):
                    gp.dma_start(s_wr[0][:], wrT[0].rearrange("(c p) j -> p c j", p=128)).then_inc(lw.h, 16); lw.inc(16)
                    gp.dma_start(s_bre[0][:], br_e[0].unsqueeze(0)).then_inc(lw.h, 16)
                wr_need[0] = lw.inc(16)

        for t in range(4 if PH >= 2 else 0):
            with nc.Block() as k:
                @k.sync
                def _(sync, g=g, t=t):
                    r = g * 512 + t * 128
                    sync.dma_start(s_phit[:], phi_bf[r:r + 128, :]).then_inc(ld.h, 16); ld.inc(16)
                    sync.wait_ge(ld.h, ld.n)
            with nc.Block() as k:
                @k.vector
                def _(vector, t=t):
                    vector.tensor_copy(s_p32[:], s_phit[:])
                    vector.drain()
                    vector.tensor_tensor(s_p32[:], s_p32[:], s_Dbig[:], op=ALU.mult)
                    vector.drain()
                    vector.tensor_reduce(s_den.ap().unsqueeze(2),
                                         s_p32.ap().rearrange("p (h f) -> p h f", h=16),
                                         axis=AX.X, op=ALU.add)
                    vector.drain()
                    vector.tensor_tensor(s_den[:], s_den[:], s_SepsB[:], op=ALU.add)
                    vector.drain()
                    vector.reciprocal(s_rden[:], s_den[:])
            with nc.Block() as k:
                @k.tensor
                def _(tensor, t=t):
                    tensor.transpose(ps_t[:], s_rden[:], s_ident[:])
                    for kc in range(8):
                        tensor.matmul(ps_r[:], s_xnfm[:, kc, t * 128:(t + 1) * 128],
                                      s_centT[:, kc, :], start=(kc == 0), stop=False,
                                      skip_group_check=True)
                    tensor.matmul(ps_r[:], s_ones[:, 0:128], s_ebrow[:],
                                  start=False, stop=True, skip_group_check=True)
            with nc.Block() as k:
                @k.vector
                def _(vector, t=t):
                    vector.tensor_copy(s_rdT[:], ps_t[:])
                    vector.tensor_copy(s_lg[:], ps_r[:])
                    vector.drain()
                    vector.tensor_copy(s_rdT_bf[:], s_rdT[:])
                    vector.max(out=s_m8[:], in_=s_lg[:])
                    vector.drain()
                    vector.tensor_scalar(s_m1n[:], s_m8[:, 0:1], -1.0, None,
                                         op0=ALU.mult)
                    vector.tensor_tensor(s_msk[:], s_lg[:],
                                         s_m8[:, 1:2].broadcast_to([128, 16]),
                                         op=ALU.is_ge).then_inc(vv.h, 1); vv.inc()

                @k.scalar
                def _(scalar):
                    scalar.wait_ge(vv.h, vv.n)
                    scalar.activation(s_er[:], s_lg[:], AF.Exp, bias=s_m1n[:],
                                      accum_out=s_sum[:]).then_inc(sc.h, 1); sc.inc()

                @k.vector
                def _(vector, t=t):
                    vector.wait_ge(sc.h, sc.n)
                    vector.reciprocal(s_rs[:], s_sum[:])
                    vector.tensor_tensor(s_er[:], s_er[:], s_msk[:], op=ALU.mult)
                    vector.drain()
                    vector.tensor_scalar(s_wcol[:, t, :], s_er[:], s_rs[:], None,
                                         op0=ALU.mult)
            if PH >= 3:
                with nc.Block() as k:
                    tv0, sc0, vv0 = tv.n, sc.n, vv.n

                    @k.tensor
                    def _(tensor, t=t, tv0=tv0, sc0=sc0, vv0=vv0):
                        for p in range(8):
                            if p >= 1:
                                tensor.wait_ge(sc.h, sc0 + p)
                                tensor.wait_ge(vv.h, vv0 + p)
                            tensor.matmul(ps_e[:], s_selp[:, p, :], s_rdT_bf[:],
                                          start=True, stop=True, skip_group_check=True)
                            tensor.matmul(ps_n[:], s_Npair[:, p, :],
                                          s_phifm[:, p, t * 128:(t + 1) * 128],
                                          start=True, stop=True,
                                          skip_group_check=True).then_inc(tv.h, 1); tv.inc()
                        tensor.wait_ge(vv.h, vv0 + 8)
                        for jh in range(2):
                            pd = ps_a if jh == 0 else ps_b
                            for kc in range(8):
                                tensor.matmul(pd[:], s_ctx[:, kc, :],
                                              s_woT[:, kc, jh * 512:(jh + 1) * 512],
                                              start=(kc == 0), stop=False, skip_group_check=True)
                            tensor.matmul(pd[:], s_ones[:, 0:128],
                                          s_borow[:, jh * 512:(jh + 1) * 512],
                                          start=False, stop=True,
                                          skip_group_check=True).then_inc(tv.h, 1); tv.inc()

                    @k.scalar
                    def _(scalar, tv0=tv0, vv0=vv0):
                        for p in range(8):
                            scalar.wait_ge(tv.h, tv0 + p + 1)
                            if p >= 1:
                                scalar.wait_ge(vv.h, vv0 + p)
                            scalar.copy(s_dex[:], ps_e[:]).then_inc(sc.h, 1); sc.inc()

                    @k.vector
                    def _(vector, t=t, tv0=tv0, sc0=sc0):
                        for p in range(8):
                            vector.wait_ge(sc.h, sc0 + p + 1)
                            vector.tensor_tensor(s_ctx[:, p, :], ps_n[:], s_dex[:],
                                                 op=ALU.mult).then_inc(vv.h, 1); vv.inc()
                            vector.drain()
                        vector.wait_ge(tv.h, tv0 + 10)
                        vector.tensor_copy(s_attn[:, t, 0:512], ps_a[:])
                        vector.tensor_copy(s_attn[:, t, 512:1024], ps_b[:])

        with nc.Block() as k:
            @k.vector
            def _(vector):
                vector.memset(s_acc[:], 0.0)
        for e in range(16 if PH >= 4 else 0):
            with nc.Block() as k:
                @k.gpsimd
                def _(gp, g=g, e=e):
                    gp.wait_ge(lw.h, lw.n)
                    if e < 15:
                        gp.dma_start(s_wr[(e + 1) % 2][:],
                                     wrT[e + 1].rearrange("(c p) j -> p c j", p=128)
                                     ).then_inc(lw.h, 16); lw.inc(16)
                        gp.dma_start(s_bre[(e + 1) % 2][:],
                                     br_e[e + 1].unsqueeze(0)).then_inc(lw.h, 16)
                        wr_ready[(e + 1) % 2] = lw.inc(16)
                    elif g < NGR - 1:
                        gp.dma_start(s_wr[0][:],
                                     wrT[0].rearrange("(c p) j -> p c j", p=128)
                                     ).then_inc(lw.h, 16); lw.inc(16)
                        gp.dma_start(s_bre[0][:],
                                     br_e[0].unsqueeze(0)).then_inc(lw.h, 16)
                        wr_ready[0] = lw.inc(16)

                @k.tensor
                def _(tensor, e=e):
                    tensor.wait_ge(lw.h, wr_need[0])
                    moe_mm(tensor, s_wr[e % 2], e)

                @k.vector
                def _(vector, e=e):
                    moe_acc(vector, e)
            wr_need[0] = wr_ready[(e + 1) % 2]
        if PH >= 4:
            with nc.Block() as k:
                @k.tensor
                def _(tensor):
                    moe_mm(tensor, s_wsT2, None)

                @k.vector
                def _(vector):
                    moe_acc(vector, None)

        for t in range(4 if PH >= 5 else 0):
            with nc.Block() as k:
                sgn = sg.n

                @k.sync
                def _(sync, g=g, t=t):
                    r = g * 512 + t * 128
                    sync.dma_start(s_xt[:], x[r:r + 128, :]).then_inc(ld.h, 16); ld.inc(16)
                    sync.dma_start(s_xnt[:], xn_bf[r:r + 128, :]).then_inc(ld.h, 16); ld.inc(16)
                    sync.wait_ge(ld.h, ld.n)

                @k.vector
                def _(vector, t=t, sgn=sgn):
                    vector.wait_ge(ld.h, ld.n)
                    vector.wait_ge(sg.h, sgn)
                    vector.tensor_copy(s_x1t[:], s_xnt[:])
                    vector.drain()
                    vector.tensor_tensor(s_x1t[:], s_x1t[:], s_xt[:], op=ALU.add)
                    vector.drain()
                    vector.tensor_tensor(s_x1t[:], s_x1t[:], s_attn[:, t, :], op=ALU.add)
                    vector.drain()
                    vector.tensor_tensor(s_x1t[:], s_x1t[:], s_acc[:, t, :], op=ALU.add)
            with nc.Block() as k:
                sc0, vv0 = sc.n, vv.n

                @k.vector
                def _(vector, sc0=sc0):
                    emit_ln_vector(vector, sc, vv, s_x1t[:], s_xn1t[:], s_sq, s_st,
                                   sc0).then_inc(vv.h, 1)
                vv.inc()

                @k.scalar
                def _(scalar, vv0=vv0):
                    emit_ln_scalar(scalar, sc, vv, s_x1t[:], s_sq, s_st, vv0)

                @k.gpsimd
                def _(gp, g=g, t=t):
                    gp.wait_ge(vv.h, vv.n)
                    r = g * 512 + t * 128
                    gp.dma_start(x1[r:r + 128, :], s_x1t[:]).then_inc(sg.h, 16); sg.inc(16)
                    gp.dma_start(xn1_bf[r:r + 128, :], s_xn1t[:]).then_inc(sg.h, 16); sg.inc(16)
    with nc.Block() as k:
        @k.gpsimd
        def _(gp):
            gp.wait_ge(sg.h, sg.n)
    return b


def build_launch3a():
    b = B()
    nc = b.nc
    NGR = T // 512
    xn1_bf = b.din("xn1_bf", [T, D], BF16).ap()
    wq1T = b.din("wq1T", [D, D], F32).ap()
    wk1T = b.din("wk1T", [D, D], F32).ap()
    wv1T = b.din("wv1T", [D, D], F32).ap()
    wg1T = b.din("wg1T", [D, 128], F32).ap()
    wg2T = b.din("wg2T", [128, D], F32).ap()
    identf = b.din("identf", [128, 128], F32).ap()

    q_fm = b.dout("q_fm", [D, T], BF16).ap()
    w_fm = b.dout("w_fm", [D, T], F32).ap()
    E_fm = b.dout("E_fm", [D, T], F32).ap()
    v_tm = b.dout("v_tm", [T, D], BF16).ap()
    gate_tm = b.dout("gate_tm", [T, D], BF16).ap()
    Wc = b.dout("Wc", [8, 128], F32).ap()
    Clin = b.dout("Clin", [8, 128, 128], F32).ap()

    s_wq1T = b.sb("s_wq1T", [128, 8, D], BF16)
    s_wk1T = b.sb("s_wk1T", [128, 8, D], BF16)
    s_wv1T = b.sb("s_wv1T", [128, 8, D], BF16)
    s_wg1T = b.sb("s_wg1T", [128, 8, 128], BF16)
    s_wg2T = b.sb("s_wg2T", [128, D], BF16)
    s_idbf = b.sb("s_idbf", [128, 128], BF16)

    s_xnfm = b.sb("s3_xnfm", [128, 8, 512], BF16)
    s_qg = b.sb("s_qg", [128, 8, 512], BF16)
    s_wf = b.sb("s_wf", [128, 8, 512], F32)
    s_wb = b.sb("s_wb", [128, 8, 512], BF16)
    s_Ef = b.sb("s_Ef", [128, 8, 512], F32)
    s_Ecar = b.sb("s_Ecar", [128, 8], F32)
    s_vt = b.sb("s3_vt", [128, D], BF16)
    s_xgt = b.sb("s_xgt", [128, 128], BF16)
    s_xgT = b.sb("s_xgT", [128, 128], BF16)
    s_gt = b.sb("s_gt", [128, D], BF16)
    s_wtm = b.sb("s_wtm", [128, 128], BF16)
    s_Cout = b.sb("s_Cout", [128, 8, 128], F32)

    ps_a = b.ps("ps3_a", [128, 512])
    ps_b = b.ps("ps3_b", [128, 512])
    ps_g = b.ps("ps3_g", [128, 128])
    ps_t = b.ps("ps3_t", [128, 128], BF16)
    ps_C = b.ps("ps3_C", [128, 1024])

    ld = b.sem("ld")
    lg = b.sem("lg")
    ssy = b.sem("ssy")
    sg = b.sem("sg")
    tv = b.sem("tv")
    vv = b.sem("vv")
    sc = b.sem("sc")

    with nc.Block() as k:
        @k.gpsimd
        def _(g):
            for dst, src in ((s_wq1T, wq1T), (s_wk1T, wk1T), (s_wv1T, wv1T)):
                g.dma_start(dst[:], src.rearrange("(c p) j -> p c j", p=128)).then_inc(lg.h, 16); lg.inc(16)
            g.dma_start(s_wg1T[:], wg1T.rearrange("(c p) j -> p c j", p=128)).then_inc(lg.h, 16); lg.inc(16)
            g.dma_start(s_wg2T[:], wg2T).then_inc(lg.h, 16); lg.inc(16)
            g.dma_start(s_idbf[:], identf).then_inc(lg.h, 16); lg.inc(16)
            g.wait_ge(lg.h, lg.n)

        @k.vector
        def _(vector):
            vector.memset(s_Ecar[:], 0.0)

    for g in range(NGR):
        r0 = g * 512
        with nc.Block() as k:
            @k.sync
            def _(sync, g=g, r0=r0):
                for kc in range(8):
                    sync.dma_start_transpose(
                        s_xnfm[:, kc, :], xn1_bf[r0:r0 + 512, kc * 128:(kc + 1) * 128]
                    ).then_inc(ld.h, 16); ld.inc(16)
                sync.wait_ge(ld.h, ld.n)

        for which in range(2):
            src_w = s_wq1T if which == 0 else s_wk1T
            with nc.Block() as k:
                tv0, sc0 = tv.n, sc.n

                @k.tensor
                def _(tensor, which=which, src_w=src_w, tv0=tv0, sc0=sc0):
                    for j in range(8):
                        pd = ps_a if j % 2 == 0 else ps_b
                        if j >= 2:
                            tensor.wait_ge(sc.h, sc0 + j - 1)
                        for kc in range(8):
                            mm = tensor.matmul(pd[:], src_w[:, kc, j * 128:(j + 1) * 128],
                                               s_xnfm[:, kc, :], start=(kc == 0), stop=(kc == 7),
                                               skip_group_check=True)
                        mm.then_inc(tv.h, 1); tv.inc()

                @k.scalar
                def _(scalar, which=which, tv0=tv0):
                    for j in range(8):
                        pd = ps_a if j % 2 == 0 else ps_b
                        scalar.wait_ge(tv.h, tv0 + j + 1)
                        if which == 0:
                            scalar.activation(s_qg[:, j, :], pd[:], AF.Silu).then_inc(sc.h, 1); sc.inc()
                        else:
                            scalar.activation(s_wf[:, j, :], pd[:], AF.Exp).then_inc(sc.h, 1); sc.inc()
        with nc.Block() as k:
            @k.vector
            def _(vector):
                vector.tensor_copy(s_wb[:], s_wf[:])
                for kc in range(8):
                    vector.drain()
                    vector.tensor_tensor_scan(s_Ef[:, kc, :], s_wf[:, kc, :],
                                              s_wf[:, kc, :], s_Ecar[:, kc:kc + 1],
                                              op0=ALU.add, op1=ALU.bypass)
                    vector.drain()
                    vector.tensor_copy(s_Ecar[:, kc:kc + 1], s_Ef[:, kc, 511:512])

        with nc.Block() as k:
            @k.gpsimd
            def _(gp, g=g, r0=r0):
                gp.dma_start(q_fm.rearrange("(c p) t -> p c t", p=128)[:, :, r0:r0 + 512],
                             s_qg[:]).then_inc(sg.h, 16); sg.inc(16)
                gp.dma_start(w_fm.rearrange("(c p) t -> p c t", p=128)[:, :, r0:r0 + 512],
                             s_wf[:]).then_inc(sg.h, 16); sg.inc(16)
                gp.dma_start(E_fm.rearrange("(c p) t -> p c t", p=128)[:, :, r0:r0 + 512],
                             s_Ef[:]).then_inc(sg.h, 16); sg.inc(16)
                gp.wait_ge(sg.h, sg.n)

        for t in range(4):
            i = g * 4 + t
            with nc.Block() as k:
                @k.tensor
                def _(tensor, t=t):
                    for jh in range(2):
                        pd = ps_a if jh == 0 else ps_b
                        for kc in range(8):
                            tensor.matmul(pd[:], s_xnfm[:, kc, t * 128:(t + 1) * 128],
                                          s_wv1T[:, kc, jh * 512:(jh + 1) * 512],
                                          start=(kc == 0), stop=(kc == 7),
                                          skip_group_check=True)
                    for kc in range(8):
                        tensor.matmul(ps_g[:], s_xnfm[:, kc, t * 128:(t + 1) * 128],
                                      s_wg1T[:, kc, :], start=(kc == 0), stop=(kc == 7),
                                      skip_group_check=True)
            with nc.Block() as k:
                @k.vector
                def _(vector):
                    vector.tensor_copy(s_vt[:, 0:512], ps_a[:])
                    vector.tensor_copy(s_vt[:, 512:1024], ps_b[:])
                    vector.tensor_copy(s_xgt[:], ps_g[:])
            with nc.Block() as k:
                @k.tensor
                def _(tensor):
                    tensor.transpose(ps_t[:], s_xgt[:], s_idbf[:])
            with nc.Block() as k:
                @k.vector
                def _(vector):
                    vector.tensor_copy(s_xgT[:], ps_t[:])
            with nc.Block() as k:
                @k.tensor
                def _(tensor):
                    for jh in range(2):
                        pd = ps_a if jh == 0 else ps_b
                        tensor.matmul(pd[:], s_xgT[:], s_wg2T[:, jh * 512:(jh + 1) * 512],
                                      start=True, stop=True, skip_group_check=True)
            with nc.Block() as k:
                @k.scalar
                def _(scalar):
                    scalar.activation(s_gt[:, 0:512], ps_a[:], AF.Silu)
                    scalar.activation(s_gt[:, 512:1024], ps_b[:], AF.Silu)
            for h in range(8):
                with nc.Block() as k:
                    @k.tensor
                    def _(tensor, h=h, t=t):
                        tensor.transpose(ps_t[:], s_wb[:, h, t * 128:(t + 1) * 128],
                                         s_idbf[:])
                with nc.Block() as k:
                    @k.vector
                    def _(vector):
                        vector.tensor_copy(s_wtm[:], ps_t[:])
                with nc.Block() as k:
                    @k.tensor
                    def _(tensor, h=h, i=i):
                        tensor.matmul(ps_C[:, h * 128:(h + 1) * 128], s_wtm[:],
                                      s_vt[:, h * 128:(h + 1) * 128],
                                      start=(i == 0 and h % 4 == 0), stop=(i == NT - 1),
                                      skip_group_check=True)
            with nc.Block() as k:
                @k.gpsimd
                def _(gp, i=i):
                    gp.dma_start(v_tm[i * 128:(i + 1) * 128, :], s_vt[:]).then_inc(sg.h, 16); sg.inc(16)
                    gp.dma_start(gate_tm[i * 128:(i + 1) * 128, :], s_gt[:]).then_inc(sg.h, 16); sg.inc(16)
                    gp.wait_ge(sg.h, sg.n)

    with nc.Block() as k:
        @k.vector
        def _(vector):
            vector.tensor_copy(s_Cout.ap().rearrange("p h c -> p (h c)"), ps_C[:])
    with nc.Block() as k:
        @k.sync
        def _(sync):
            sync.dma_start(Clin.transpose([1, 0, 2]), s_Cout[:]).then_inc(ssy.h, 16); ssy.inc(16)
            with nc.allow_non_contiguous_dma(reason="tiny"):
                sync.dma_start(Wc.transpose([1, 0]), s_Ecar[:]).then_inc(ssy.h, 16); ssy.inc(16)
            sync.wait_ge(ssy.h, ssy.n)
    return b


def build_launch3b():
    b = B()
    nc = b.nc
    q_fm = b.din("q_fm", [D, T], BF16).ap()
    w_fm = b.din("w_fm", [D, T], F32).ap()
    E_fm = b.din("E_fm", [D, T], F32).ap()
    v_tm = b.din("v_tm", [T, D], BF16).ap()
    gate_tm = b.din("gate_tm", [T, D], BF16).ap()
    Wc4 = b.din("Wc4", [4, 8, 128], F32).ap()
    Clin4 = b.din("Clin4", [4, 8, 128, 128], F32).ap()
    mask4 = b.din("mask4", [1, 4], F32).ap()
    segflag = b.din("segflag", [1, 1], F32).ap()
    wo1T = b.din("wo1T", [D, D], F32).ap()
    gn_row = b.din("gn_row", [1, D], F32).ap()
    identf = b.din("identf", [128, 128], F32).ap()
    ut = b.din("ut", [128, 128], F32).ap()

    ya = b.dout("ya", [T, D], F32).ap()

    s_wo1T = b.sb("s_wo1T", [128, 8, D], BF16)
    s_idbf = b.sb("s3b_idbf", [128, 128], BF16)
    s_ut = b.sb("s_ut", [128, 128], BF16)
    s_gn = b.sb("s_gn", [128, D], F32)
    s_maskB = b.sb("s_maskB", [128, 4], F32)
    s_flag = b.sb("s_flag", [128, 1], F32)
    s_Wc4 = b.sb("s_Wc4", [128, 4, 8], F32)
    s_Ein = b.sb("s_Ein", [128, 8], F32)
    s_rEin = b.sb("s_rEin", [128, 8], F32)
    s_Cld = b.sb("s_Cld", [128, 8, 128], F32)
    s_Sf = b.sb("s_Sf", [128, 8, 128], F32)
    s_Sbf = b.sb("s_Sbf", [128, 8, 128], BF16)
    s_Eprev = b.sb("s_Eprev", [128, 8], F32)
    s_w0col = b.sb("s_w0col", [128, 8], F32)
    s_tmp8 = b.sb("s_tmp8", [128, 8], F32)

    s_qc = b.sb("s_qc", [128, 8, 128], BF16)
    s_wc = b.sb("s_wc", [128, 8, 128], F32)
    s_Ec = b.sb("s_Ec", [128, 8, 128], F32)
    s_vc = b.sb("s_vc", [128, D], BF16)
    s_gc = b.sb("s_gc", [128, D], BF16)
    s_rEgB = b.sb("s_rEgB", [128, 8, 128], F32)
    s_q2B = b.sb("s_q2B", [128, 8, 128], BF16)
    s_k2B = b.sb("s_k2B", [128, 8, 128], BF16)
    s_kTfB = b.sb("s_kTfB", [128, 8, 128], BF16)
    s_kTtB = b.sb("s_kTtB", [128, 8, 128], BF16)
    s_AbB = b.sb("s_AbB", [128, 8, 128], BF16)
    s_dec = b.sb("s_dec", [128, 8], F32)
    s_rEp = b.sb("s_rEp", [128, 8], F32)
    s_rEp2 = b.sb("s_rEp2", [128, 8], F32)
    s_t0B = b.sb("s_t0B", [128, 8, 128], F32)
    s_o = b.sb("s_o", [128, D], F32)
    s_osq = b.sb("s_osq", [128, D], F32)
    s_ost = b.sb("s_ost", [128, 8], F32)
    s_gc32 = b.sb("s_gc32", [128, D], F32)
    s_rgb = b.sb("s_rgb", [128, D], BF16)
    s_rgT = b.sb("s_rgT", [128, 8, 128], BF16)
    s_yat = b.sb("s_yat", [128, D], F32)

    ps_o = b.ps("ps3b_o", [128, 1024])
    ps_AY = b.ps("ps3b_AY", [128, 1024])
    ps_S = b.ps("ps3b_S", [128, 1024])
    ps_T = b.ps("ps3b_T", [128, 1024], BF16)

    ld = b.sem("ld")
    lg = b.sem("lg")
    sg = b.sem("sg")
    tv = b.sem("tv")
    vv = b.sem("vv")
    sc = b.sem("sc")

    with nc.Block() as k:
        @k.sync
        def _(sync):
            sync.dma_start(s_gn[:], gn_row.broadcast_to([128, D])).then_inc(ld.h, 16); ld.inc(16)
            sync.dma_start(s_maskB[:], mask4.broadcast_to([128, 4])).then_inc(ld.h, 16); ld.inc(16)
            sync.dma_start(s_flag[:], segflag.broadcast_to([128, 1])).then_inc(ld.h, 16); ld.inc(16)
            sync.dma_start(s_Wc4[:], Wc4.transpose([2, 0, 1]), allow_slow_non_contiguous=True).then_inc(ld.h, 16); ld.inc(16)
            with nc.allow_non_contiguous_dma(reason="tiny w0col"):
                sync.dma_start(s_w0col[:], w_fm.rearrange("(c p) t -> p t c", p=128)[:, 0:1, :].squeeze(1)).then_inc(ld.h, 16); ld.inc(16)
            sync.wait_ge(ld.h, ld.n)

        @k.gpsimd
        def _(g):
            g.dma_start(s_wo1T[:], wo1T.rearrange("(c p) j -> p c j", p=128)).then_inc(lg.h, 16); lg.inc(16)
            g.dma_start(s_idbf[:], identf).then_inc(lg.h, 16); lg.inc(16)
            g.dma_start(s_ut[:], ut).then_inc(lg.h, 16); lg.inc(16)
            g.wait_ge(lg.h, lg.n)
    with nc.Block() as k:
        @k.vector
        def _(vector):
            vector.tensor_tensor(s_Wc4.ap(), s_Wc4.ap(),
                                 s_maskB.ap().unsqueeze(2).broadcast_to([128, 4, 8]),
                                 op=ALU.mult)
            vector.drain()
            vector.tensor_reduce(s_Ein.ap().unsqueeze(2),
                                 s_Wc4.ap().transpose([0, 2, 1]), axis=AX.X, op=ALU.add)
            vector.drain()
            vector.tensor_tensor(s_tmp8[:], s_Ein[:],
                                 s_flag.ap().broadcast_to([128, 8]), op=ALU.add)
            vector.drain()
            vector.reciprocal(s_rEin[:], s_tmp8[:])
            vector.drain()
            vector.tensor_tensor(s_tmp8[:], s_w0col[:],
                                 s_flag.ap().broadcast_to([128, 8]), op=ALU.mult)
            vector.drain()
            vector.tensor_tensor(s_Eprev[:], s_Ein[:], s_tmp8[:], op=ALU.add)
            vector.memset(s_Sf[:], 0.0)
    for c4 in range(4):
        with nc.Block() as k:
            @k.sync
            def _(sync, c4=c4):
                sync.dma_start(s_Cld[:], Clin4[c4].transpose([1, 0, 2])).then_inc(ld.h, 16); ld.inc(16)
                sync.wait_ge(ld.h, ld.n)
        with nc.Block() as k:
            @k.vector
            def _(vector, c4=c4):
                vector.scalar_tensor_tensor(s_Sf.ap().rearrange("p h c -> p (h c)"),
                                            s_Cld.ap().rearrange("p h c -> p (h c)"),
                                            s_maskB[:, c4:c4 + 1],
                                            s_Sf.ap().rearrange("p h c -> p (h c)"),
                                            op0=ALU.mult, op1=ALU.add)
                vector.drain()
    with nc.Block() as k:
        @k.vector
        def _(vector):
            for h in range(8):
                vector.tensor_scalar(s_Sf[:, h, :], s_Sf[:, h, :],
                                     s_rEin[:, h:h + 1], None, op0=ALU.mult)
                vector.drain()
            vector.tensor_copy(s_Sbf[:], s_Sf[:])

    for i in range(NT):
        r = i * 128
        with nc.Block() as k:
            @k.sync
            def _(sync, r=r):
                sync.dma_start(s_qc[:], q_fm.rearrange("(c p) t -> p c t", p=128)[:, :, r:r + 128]).then_inc(ld.h, 16); ld.inc(16)
                sync.dma_start(s_wc[:], w_fm.rearrange("(c p) t -> p c t", p=128)[:, :, r:r + 128]).then_inc(ld.h, 16); ld.inc(16)
                sync.dma_start(s_Ec[:], E_fm.rearrange("(c p) t -> p c t", p=128)[:, :, r:r + 128]).then_inc(ld.h, 16); ld.inc(16)
                sync.dma_start(s_vc[:], v_tm[r:r + 128, :]).then_inc(ld.h, 16); ld.inc(16)
                sync.dma_start(s_gc[:], gate_tm[r:r + 128, :]).then_inc(ld.h, 16); ld.inc(16)
                sync.wait_ge(ld.h, ld.n)
        with nc.Block() as k:
            @k.vector
            def _(vector):
                vector.tensor_tensor(s_t0B.ap(), s_Ec.ap(),
                                     s_Ein.ap().unsqueeze(2).broadcast_to([128, 8, 128]),
                                     op=ALU.add)
                vector.drain()
                vector.reciprocal(s_rEgB.ap().rearrange("p h c -> p (h c)"),
                                  s_t0B.ap().rearrange("p h c -> p (h c)"))
                vector.drain()
                vector.tensor_copy(s_rEp2[:], s_Eprev[:])
                vector.drain()
                vector.tensor_copy(s_Eprev.ap().unsqueeze(2), s_t0B[:, :, 127:128])
                vector.drain()
                vector.tensor_tensor(s_t0B.ap(), s_qc.ap(),
                                     s_rEp2.ap().unsqueeze(2).broadcast_to([128, 8, 128]),
                                     op=ALU.mult)
                vector.drain()
                vector.tensor_tensor(s_q2B.ap(), s_t0B.ap(), s_rEgB.ap(), op=ALU.mult)
                vector.reciprocal(s_rEp[:], s_rEp2[:])
                vector.drain()
                vector.tensor_tensor(s_k2B.ap(), s_wc.ap(),
                                     s_rEp.ap().unsqueeze(2).broadcast_to([128, 8, 128]),
                                     op=ALU.mult)
                vector.tensor_tensor(s_kTfB.ap(), s_wc.ap(),
                                     s_rEgB[:, :, 127:128].broadcast_to([128, 8, 128]),
                                     op=ALU.mult)
                vector.tensor_tensor(s_dec.ap().unsqueeze(2), s_rEp2.ap().unsqueeze(2),
                                     s_rEgB[:, :, 127:128], op=ALU.mult)
                vector.drain()
        with nc.Block() as k:
            @k.tensor
            def _(tensor):
                for h in range(8):
                    tensor.transpose(ps_T[:, h * 128:(h + 1) * 128], s_kTfB[:, h, :], s_idbf[:])
                for h in range(8):
                    tensor.matmul(ps_o[:, h * 128:(h + 1) * 128], s_q2B[:, h, :],
                                  s_Sbf[:, h, :], start=(h % 4 == 0), stop=False,
                                  skip_group_check=True)
                for h in range(8):
                    tensor.matmul(ps_AY[:, h * 128:(h + 1) * 128], s_k2B[:, h, :],
                                  s_q2B[:, h, :], start=(h % 4 == 0), stop=True,
                                  skip_group_check=True)
        with nc.Block() as k:
            @k.vector
            def _(vector):
                vector.tensor_copy(s_kTtB.ap().rearrange("p h c -> p (h c)"), ps_T[:])
                vector.tensor_tensor(s_AbB.ap(), ps_AY.ap().rearrange("p (h c) -> p h c", h=8),
                                     s_ut.ap().unsqueeze(1).broadcast_to([128, 8, 128]),
                                     op=ALU.mult)
        with nc.Block() as k:
            @k.tensor
            def _(tensor):
                for h in range(8):
                    tensor.matmul(ps_o[:, h * 128:(h + 1) * 128], s_AbB[:, h, :],
                                  s_vc[:, h * 128:(h + 1) * 128], start=False, stop=True,
                                  skip_group_check=True)
                for h in range(8):
                    tensor.matmul(ps_S[:, h * 128:(h + 1) * 128], s_kTtB[:, h, :],
                                  s_vc[:, h * 128:(h + 1) * 128], start=(h % 4 == 0),
                                  stop=True, skip_group_check=True)
        with nc.Block() as k:
            @k.vector
            def _(vector):
                vector.tensor_tensor(s_Sf.ap(), s_Sf.ap(),
                                     s_dec.ap().unsqueeze(2).broadcast_to([128, 8, 128]),
                                     op=ALU.mult)
                vector.drain()
                vector.tensor_tensor(s_Sf.ap().rearrange("p h c -> p (h c)"),
                                     s_Sf.ap().rearrange("p h c -> p (h c)"),
                                     ps_S[:], op=ALU.add)
                vector.drain()
                vector.tensor_copy(s_Sbf[:], s_Sf[:])
                vector.drain()
        with nc.Block() as k:
            @k.scalar
            def _(scalar):
                scalar.copy(s_o[:, 0:512], ps_o[:, 0:512])
                scalar.copy(s_o[:, 512:1024], ps_o[:, 512:1024])
        with nc.Block() as k:
            sc0, vv0 = sc.n, vv.n

            @k.scalar
            def _(scalar, sc0=sc0, vv0=vv0):
                scalar.square(s_osq[:], s_o[:]).then_inc(sc.h, 1); sc.inc()
                scalar.wait_ge(vv.h, vv0 + 1)
                scalar.activation(s_ost[:, 2:3], s_ost[:, 1:2], AF.Sqrt).then_inc(sc.h, 1); sc.inc()

            @k.vector
            def _(vector, sc0=sc0):
                vector.wait_ge(sc.h, sc0 + 1)
                vector.tensor_reduce(s_ost[:, 0:1], s_osq[:], axis=AX.X, op=ALU.add)
                vector.drain()
                vector.tensor_scalar(s_ost[:, 1:2], s_ost[:, 0:1], 1.0 / D, None, op0=ALU.mult)
                vector.drain()
                vector.tensor_scalar(s_ost[:, 1:2], s_ost[:, 1:2], RMS_EPS, None,
                                     op0=ALU.add).then_inc(vv.h, 1); vv.inc()
                vector.wait_ge(sc.h, sc0 + 2)
                vector.reciprocal(s_ost[:, 3:4], s_ost[:, 2:3])
                vector.drain()
                vector.tensor_copy(s_gc32[:], s_gc[:])
                vector.drain()
                vector.tensor_scalar(s_o[:], s_o[:], s_ost[:, 3:4], None, op0=ALU.mult)
                vector.drain()
                vector.tensor_tensor(s_o[:], s_o[:], s_gn[:], op=ALU.mult)
                vector.drain()
                vector.tensor_tensor(s_rgb[:], s_o[:], s_gc32[:], op=ALU.mult)
        with nc.Block() as k:
            @k.tensor
            def _(tensor):
                for kc in range(8):
                    tensor.transpose(ps_T[:, kc * 128:(kc + 1) * 128],
                                     s_rgb[:, kc * 128:(kc + 1) * 128], s_idbf[:])
        with nc.Block() as k:
            @k.vector
            def _(vector):
                vector.tensor_copy(s_rgT.ap().rearrange("p c t -> p (c t)"), ps_T[:])
        with nc.Block() as k:
            @k.tensor
            def _(tensor):
                for jh in range(2):
                    for kc in range(8):
                        tensor.matmul(ps_AY[:, jh * 512:(jh + 1) * 512], s_rgT[:, kc, :],
                                      s_wo1T[:, kc, jh * 512:(jh + 1) * 512],
                                      start=(kc == 0), stop=(kc == 7),
                                      skip_group_check=True)
        with nc.Block() as k:
            @k.vector
            def _(vector, i=i):
                vector.wait_ge(sg.h, sg.n)
                vector.tensor_copy(s_yat[:, 0:512], ps_AY[:, 0:512])
                vector.tensor_copy(s_yat[:, 512:1024],
                                   ps_AY[:, 512:1024]).then_inc(vv.h, 1)
            vv.inc()

            @k.gpsimd
            def _(gp, i=i, r=r):
                gp.wait_ge(vv.h, vv.n)
                gp.dma_start(ya[r:r + 128, :], s_yat[:]).then_inc(sg.h, 16); sg.inc(16)
    with nc.Block() as k:
        @k.gpsimd
        def _(gp):
            gp.wait_ge(sg.h, sg.n)
    return b


def build_launch4():
    b = B()
    nc = b.nc
    NGR = T // 512
    x1 = b.din("x1", [T, D], F32).ap()
    ya = b.din("ya", [T, D], F32).ap()
    xn1_bf = b.din("xn1_bf", [T, D], BF16).ap()
    wr1 = b.din("wr1", [D, 8], F32).ap()
    br1_row = b.din("br1_row", [1, 8], F32).ap()
    w1 = b.din("w1", [8, D, 4096], F32).ap()
    b1_ = b.din("b1_", [8, 4096], F32).ap()
    w2 = b.din("w2", [8, 4096, D], F32).ap()
    b2_ = b.din("b2_", [8, D], F32).ap()
    ones_row = b.din("ones_row", [1, 128], F32).ap()

    out = b.dout("out", [T, D], F32).ap()
    acc = b.dtmp("acc4", [T, D], F32).ap()

    s_wr1 = b.sb("s_wr1", [128, 8, 8], BF16)
    s_br1 = b.sb("s_br1", [1, 8], BF16)
    s_ones = b.sb("s4_ones", [1, 128], BF16)
    s_w1 = b.sb("s_w1", [128, 8, 4096], BF16)
    s_w2 = b.sb("s_w2", [128, 32, D], BF16)
    s_b1c = b.sb("s_b1c", [128, 32], F32)
    s_b2r = b.sb("s_b2r", [1, D], BF16)
    s_hfm = b.sb("s_hfm", [128, 32, 512], BF16)
    s_xnfm = b.sb("s4_xnfm", [128, 8, 512], BF16)
    s_wcolB = b.sb("s_wcolB", [128, NT, 8], F32)
    s_yw = [b.sb("s_yw0", [128, D], F32), b.sb("s_yw1", [128, D], F32)]
    s_ot = [b.sb("s_ot0", [128, D], F32), b.sb("s_ot1", [128, D], F32)]
    s_xt = b.sb("s4_xt", [128, D], F32)
    s_yat = b.sb("s4_yat", [128, D], F32)
    s_base = b.sb("s4_base", [128, D], F32)
    s_lg = b.sb("s4_lg", [128, 8], F32)
    s_m8 = b.sb("s4_m8", [128, 8], F32)
    s_m1n = b.sb("s4_m1n", [128, 1], F32)
    s_er = b.sb("s4_er", [128, 8], F32)
    s_sum = b.sb("s4_sum", [128, 1], F32)
    s_rs = b.sb("s4_rs", [128, 1], F32)
    s_msk = b.sb("s4_msk", [128, 8], F32)

    ps_a = b.ps("ps4_a", [128, 512])
    ps_b = b.ps("ps4_b", [128, 512])
    ps_y = [b.ps("ps4_y0", [128, 512]), b.ps("ps4_y1", [128, 512])]
    ps_r = b.ps("ps4_r", [128, 8])

    ld = b.sem("ld")
    lg = b.sem("lg")
    lw = b.sem("lw")
    sg = b.sem("sg")
    tv = b.sem("tv")
    vv = b.sem("vv")
    sc = b.sem("sc")

    with nc.Block() as k:
        @k.gpsimd
        def _(g):
            g.dma_start(s_wr1[:], wr1.rearrange("(c p) e -> p c e", p=128)).then_inc(lg.h, 16); lg.inc(16)
            g.dma_start(s_br1[:], br1_row).then_inc(lg.h, 16); lg.inc(16)
            g.dma_start(s_ones[:], ones_row).then_inc(lg.h, 16); lg.inc(16)
            g.wait_ge(lg.h, lg.n)

    for g in range(NGR):
        r0 = g * 512
        with nc.Block() as k:
            @k.sync
            def _(sync, r0=r0):
                for kc in range(8):
                    sync.dma_start_transpose(
                        s_xnfm[:, kc, :], xn1_bf[r0:r0 + 512, kc * 128:(kc + 1) * 128]
                    ).then_inc(ld.h, 16); ld.inc(16)
                sync.wait_ge(ld.h, ld.n)
        for t in range(4):
            i = g * 4 + t
            r = i * 128
            with nc.Block() as k:
                @k.sync
                def _(sync, r=r):
                    sync.dma_start(s_xt[:], x1[r:r + 128, :]).then_inc(ld.h, 16); ld.inc(16)
                    sync.dma_start(s_yat[:], ya[r:r + 128, :]).then_inc(ld.h, 16); ld.inc(16)
                    sync.wait_ge(ld.h, ld.n)

                @k.tensor
                def _(tensor, t=t):
                    for kc in range(8):
                        tensor.matmul(ps_r[:], s_xnfm[:, kc, t * 128:(t + 1) * 128],
                                      s_wr1[:, kc, :], start=(kc == 0), stop=False,
                                      skip_group_check=True)
                    tensor.matmul(ps_r[:], s_ones[:, 0:128], s_br1[:],
                                  start=False, stop=True,
                                  skip_group_check=True).then_inc(tv.h, 1); tv.inc()
            with nc.Block() as k:
                @k.vector
                def _(vector, i=i):
                    vector.tensor_copy(s_lg[:], ps_r[:])
                    vector.drain()
                    vector.max(out=s_m8[:], in_=s_lg[:])
                    vector.drain()
                    vector.tensor_scalar(s_m1n[:], s_m8[:, 0:1], -1.0, None, op0=ALU.mult)
                    vector.tensor_tensor(s_msk[:], s_lg[:],
                                         s_m8[:, 1:2].broadcast_to([128, 8]),
                                         op=ALU.is_ge).then_inc(vv.h, 1); vv.inc()

                @k.scalar
                def _(scalar):
                    scalar.wait_ge(vv.h, vv.n)
                    scalar.activation(s_er[:], s_lg[:], AF.Exp, bias=s_m1n[:]).then_inc(sc.h, 1); sc.inc()

                @k.vector
                def _(vector, i=i):
                    vector.wait_ge(sc.h, sc.n)
                    vector.tensor_tensor(s_er[:], s_er[:], s_msk[:], op=ALU.mult)
                    vector.drain()
                    vector.tensor_reduce(s_sum[:], s_er[:], axis=AX.X, op=ALU.add)
                    vector.drain()
                    vector.reciprocal(s_rs[:], s_sum[:])
                    vector.drain()
                    vector.tensor_scalar(s_wcolB[:, i, :], s_er[:], s_rs[:], None,
                                         op0=ALU.mult)
                    vector.tensor_tensor(s_base[:], s_xt[:], s_yat[:],
                                         op=ALU.add).then_inc(vv.h, 1); vv.inc()

                @k.gpsimd
                def _(gp, r=r):
                    gp.wait_ge(vv.h, vv.n)
                    gp.dma_start(acc[r:r + 128, :], s_base[:]).then_inc(sg.h, 16); sg.inc(16)
            with nc.Block() as k:
                @k.gpsimd
                def _(gp):
                    gp.wait_ge(sg.h, sg.n)

    lw_w1 = [0] * 9
    lw_w2 = [0] * 9
    with nc.Block() as k:
        @k.gpsimd
        def _(gp):
            gp.wait_ge(lg.h, lg.n)
            gp.dma_start(s_w1[:], w1[0].rearrange("(c p) h -> p c h", p=128)).then_inc(lw.h, 16); lw.inc(16)
            gp.dma_start(s_b1c[:], b1_[0].rearrange("(c p) -> p c", p=128), allow_slow_non_contiguous=True).then_inc(lw.h, 16)
            lw_w1[0] = lw.inc(16)
            gp.wait_ge(lw.h, lw.n)
            gp.dma_start(s_w2[:], w2[0].rearrange("(c p) d -> p c d", p=128)).then_inc(lw.h, 16); lw.inc(16)
            gp.dma_start(s_b2r[:], b2_[0].unsqueeze(0)).then_inc(lw.h, 16)
            lw_w2[0] = lw.inc(16)
            gp.wait_ge(lw.h, lw.n)

        @k.sync
        def _(sync):
            for kc in range(8):
                sync.dma_start_transpose(
                    s_xnfm[:, kc, :], xn1_bf[0:512, kc * 128:(kc + 1) * 128]
                ).then_inc(ld.h, 16); ld.inc(16)
            sync.wait_ge(ld.h, ld.n)
    for e in range(8):
        for g in range(NGR):
            r0 = g * 512
            with nc.Block() as k:
                tv0, sc0 = tv.n, sc.n
                ld_full = ld.n

                @k.gpsimd
                def _(gp, e=e, g=g):
                    if g == 0 and e >= 1:
                        gp.wait_ge(lw.h, lw_w1[e])
                        gp.dma_start(s_w2[:], w2[e].rearrange("(c p) d -> p c d", p=128)).then_inc(lw.h, 16); lw.inc(16)
                        gp.dma_start(s_b2r[:], b2_[e].unsqueeze(0)).then_inc(lw.h, 16)
                        lw_w2[e] = lw.inc(16)

                @k.tensor
                def _(tensor, e=e, tv0=tv0, sc0=sc0, ld_full=ld_full):
                    tensor.wait_ge(ld.h, ld_full)
                    tensor.wait_ge(lw.h, lw_w1[e])
                    for hc in range(32):
                        pd = ps_a if hc % 2 == 0 else ps_b
                        if hc >= 2:
                            tensor.wait_ge(sc.h, sc0 + hc - 1)
                        for kc in range(8):
                            mm = tensor.matmul(pd[:], s_w1[:, kc, hc * 128:(hc + 1) * 128],
                                               s_xnfm[:, kc, :], start=(kc == 0),
                                               stop=(kc == 7), skip_group_check=True)
                        mm.then_inc(tv.h, 1); tv.inc()

                @k.scalar
                def _(scalar, e=e, tv0=tv0, ld_full=ld_full):
                    scalar.wait_ge(ld.h, ld_full)
                    scalar.wait_ge(lw.h, lw_w1[e])
                    for hc in range(32):
                        pd = ps_a if hc % 2 == 0 else ps_b
                        scalar.wait_ge(tv.h, tv0 + hc + 1)
                        scalar.activation(s_hfm[:, hc, :], pd[:], AF.Relu,
                                          bias=s_b1c[:, hc:hc + 1]).then_inc(sc.h, 1); sc.inc()
            with nc.Block() as k:
                tb, vb, sgb, ldb = tv.n, vv.n, sg.n, ld.n

                @k.sync
                def _(sync, e=e, g=g, vb=vb, sgb=sgb, ldb=ldb):
                    sync.wait_ge(sg.h, sgb)
                    for t in range(4):
                        r = g * 512 + t * 128
                        if t >= 2:
                            sync.wait_ge(vv.h, vb + 2 * t - 2)
                        sync.dma_start(s_ot[t % 2][:],
                                       acc[r:r + 128, :]).then_inc(ld.h, 16)
                        ld.inc(16)
                        sync.wait_ge(ld.h, ld.n)
                    if not (e == 7 and g == NGR - 1):
                        nr0 = ((g + 1) % NGR) * 512
                        for kc in range(8):
                            sync.dma_start_transpose(
                                s_xnfm[:, kc, :],
                                xn1_bf[nr0:nr0 + 512, kc * 128:(kc + 1) * 128]
                            ).then_inc(ld.h, 16); ld.inc(16)

                @k.tensor
                def _(tensor, e=e, tb=tb, vb=vb):
                    tensor.wait_ge(lw.h, lw_w2[e])
                    for p in range(8):
                        t, jh = p // 2, p % 2
                        if p >= 2:
                            tensor.wait_ge(vv.h, vb + p - 1)
                        for hc in range(32):
                            tensor.matmul(ps_y[p % 2][:], s_hfm[:, hc, t * 128:(t + 1) * 128],
                                          s_w2[:, hc, jh * 512:(jh + 1) * 512],
                                          start=(hc == 0), stop=False, skip_group_check=True)
                        tensor.matmul(ps_y[p % 2][:], s_ones[:, 0:128],
                                      s_b2r[:, jh * 512:(jh + 1) * 512],
                                      start=False, stop=True,
                                      skip_group_check=True).then_inc(tv.h, 1); tv.inc()

                @k.vector
                def _(vector, e=e, g=g, tb=tb, sgb=sgb, ldb=ldb, vb=vb):
                    for p in range(8):
                        t, jh = p // 2, p % 2
                        i = g * 4 + t
                        vector.wait_ge(tv.h, tb + p + 1)
                        vector.wait_ge(ld.h, ldb + (t + 1) * 16)
                        if t >= 2 and jh == 0:
                            vector.wait_ge(sg.h, sgb + (t - 1) * 16)
                        vector.scalar_tensor_tensor(
                            s_yw[t % 2][:, jh * 512:(jh + 1) * 512], ps_y[p % 2][:],
                            s_wcolB[:, i, e:e + 1],
                            s_ot[t % 2][:, jh * 512:(jh + 1) * 512],
                            op0=ALU.mult, op1=ALU.add).then_inc(vv.h, 1); vv.inc()
                        vector.drain()

                @k.gpsimd
                def _(gp, e=e, g=g, vb=vb):
                    dst = out if e == 7 else acc
                    for t in range(4):
                        r = g * 512 + t * 128
                        gp.wait_ge(vv.h, vb + 2 * t + 2)
                        gp.dma_start(dst[r:r + 128, :],
                                     s_yw[t % 2][:]).then_inc(sg.h, 16); sg.inc(16)
                        gp.wait_ge(sg.h, sg.n)
                    if g == NGR - 1 and e < 7:
                        gp.wait_ge(lw.h, lw.n)
                        gp.dma_start(s_w1[:], w1[e + 1].rearrange("(c p) h -> p c h", p=128)).then_inc(lw.h, 16); lw.inc(16)
                        gp.dma_start(s_b1c[:], b1_[e + 1].rearrange("(c p) -> p c", p=128), allow_slow_non_contiguous=True).then_inc(lw.h, 16)
                        lw_w1[e + 1] = lw.inc(16)
    with nc.Block() as k:
        @k.gpsimd
        def _(gp):
            gp.wait_ge(sg.h, sg.n)
    return b


def launch4_inputs(inp, r2, r3b):
    f32 = np.float32
    shared = dict(
        wr1=np.ascontiguousarray(np.asarray(inp["L1_ffn_Wr"])),
        br1_row=np.ascontiguousarray(np.asarray(inp["L1_ffn_br"]).reshape(1, 8)),
        w1=np.ascontiguousarray(np.asarray(inp["L1_ffn_W1"])),
        b1_=np.ascontiguousarray(np.asarray(inp["L1_ffn_b1"])),
        w2=np.ascontiguousarray(np.asarray(inp["L1_ffn_W2"])),
        b2_=np.ascontiguousarray(np.asarray(inp["L1_ffn_b2"])),
        ones_row=np.ones((1, 128), f32),
    )
    return [dict(shared, x1=r2[c]["x1"], ya=r3b[c]["ya"], xn1_bf=r2[c]["xn1_bf"])
            for c in range(NCORE)]



def _shard_tokens(arr):
    out = []
    for c in range(NCORE):
        bb, s = c // 4, c % 4
        out.append(np.ascontiguousarray(arr[bb, s * T:(s + 1) * T]))
    return out


def launch1_inputs(inp):
    f32 = np.float32
    wq = np.ascontiguousarray(inp["L0_attn_Wq"]).astype(f32, copy=False)
    wk = np.ascontiguousarray(inp["L0_attn_Wk"]).astype(f32, copy=False)
    shared = dict(
        wq_h=np.ascontiguousarray(wq.reshape(16, 64, D)),
        wk_h=np.ascontiguousarray(wk.reshape(16, 64, D)),
        w0=np.ascontiguousarray(inp["L0_attn_Wrf"][0]),
        bq2=np.ascontiguousarray(inp["L0_attn_bq"].reshape(16, 64).T),
        bk2=np.ascontiguousarray(inp["L0_attn_bk"].reshape(16, 64).T),
        wvT=np.ascontiguousarray(inp["L0_attn_Wv"].T),
        wkT=np.ascontiguousarray(wk.T),
        bv_row=np.ascontiguousarray(inp["L0_attn_bv"].reshape(1, D)),
        bk_row=np.ascontiguousarray(inp["L0_attn_bk"].reshape(1, D)),
        ones_row=np.ones((1, 128), f32),
        ones_col=np.ones((128, 1), f32),
    )
    xs = _shard_tokens(np.asarray(inp["L0_attn_x"]))
    return [dict(shared, x=xs[c]) for c in range(NCORE)]


def launch2_inputs(inp, r1):
    f32 = np.float32
    wr = np.asarray(inp["L0_ffn_Wr"])
    ws = np.asarray(inp["L0_ffn_Ws"])
    selp = np.zeros((8, 16, 128), f32)
    for p in range(8):
        selp[p, 2 * p, 0:64] = 1.0
        selp[p, 2 * p + 1, 64:128] = 1.0
    shared = dict(
        woT=np.ascontiguousarray(np.asarray(inp["L0_attn_Wo"]).T),
        bo_row=np.ascontiguousarray(np.asarray(inp["L0_attn_bo"]).reshape(1, D)),
        centT=np.ascontiguousarray(np.asarray(inp["L0_ffn_centroids"]).T),
        eb_row=np.ascontiguousarray(np.asarray(inp["L0_ffn_expert_biases"]).reshape(1, 16)),
        wrT=np.ascontiguousarray(wr.transpose(0, 2, 1)),
        br_e=np.ascontiguousarray(np.asarray(inp["L0_ffn_br"])),
        wsT=np.ascontiguousarray(ws.transpose(0, 2, 1)),
        bs2=np.ascontiguousarray(np.asarray(inp["L0_ffn_bs"]).reshape(1, 2 * D)),
        selp=selp,
        ones_row=np.ones((1, 128), f32),
        ones_col=np.ones((128, 1), f32),
        ident=np.eye(128, dtype=f32),
    )
    xs = _shard_tokens(np.asarray(inp["L0_attn_x"]))
    out = []
    for c in range(NCORE):
        bb = c // 4
        grp = range(bb * 4, bb * 4 + 4)
        out.append(dict(
            shared, x=xs[c],
            xn_bf=r1[c]["xn_bf"], phi_bf=r1[c]["phi_bf"],
            statsN4=np.stack([r1[cc]["stats_N"] for cc in grp]),
            statsD4=np.stack([r1[cc]["stats_D"] for cc in grp]),
        ))
    return out


def launch3a_inputs(inp, r2):
    f32 = np.float32
    shared = dict(
        wq1T=np.ascontiguousarray(np.asarray(inp["L1_attn_Wq"]).T),
        wk1T=np.ascontiguousarray(np.asarray(inp["L1_attn_Wk"]).T),
        wv1T=np.ascontiguousarray(np.asarray(inp["L1_attn_Wv"]).T),
        wg1T=np.ascontiguousarray(np.asarray(inp["L1_attn_Wg1"]).T),
        wg2T=np.ascontiguousarray(np.asarray(inp["L1_attn_Wg2"]).T),
        identf=np.eye(128, dtype=f32),
    )
    return [dict(shared, xn1_bf=r2[c]["xn1_bf"]) for c in range(NCORE)]


def launch3b_inputs(inp, r3a):
    f32 = np.float32
    ut = np.triu(np.ones((128, 128), f32))
    shared = dict(
        wo1T=np.ascontiguousarray(np.asarray(inp["L1_attn_Wo"]).T),
        gn_row=np.ascontiguousarray(np.asarray(inp["L1_attn_g_norm_w"]).reshape(1, D)),
        identf=np.eye(128, dtype=f32),
        ut=ut,
    )
    out = []
    for c in range(NCORE):
        bb, seg = c // 4, c % 4
        grp = range(bb * 4, bb * 4 + 4)
        mask4 = np.zeros((1, 4), f32)
        mask4[0, :seg] = 1.0
        out.append(dict(
            shared,
            q_fm=r3a[c]["q_fm"], w_fm=r3a[c]["w_fm"], E_fm=r3a[c]["E_fm"],
            v_tm=r3a[c]["v_tm"], gate_tm=r3a[c]["gate_tm"],
            Wc4=np.stack([r3a[cc]["Wc"] for cc in grp]),
            Clin4=np.stack([r3a[cc]["Clin"] for cc in grp]),
            mask4=mask4,
            segflag=np.full((1, 1), 1.0 if seg == 0 else 0.0, f32),
        ))
    return out


def kernel(**inp):
    r1 = run_bass_kernel_spmd(build_launch1().nc, launch1_inputs(inp),
                              core_ids=list(range(NCORE))).results
    r2 = run_bass_kernel_spmd(build_launch2().nc, launch2_inputs(inp, r1),
                              core_ids=list(range(NCORE))).results
    r3a = run_bass_kernel_spmd(build_launch3a().nc, launch3a_inputs(inp, r2),
                               core_ids=list(range(NCORE))).results
    r3b = run_bass_kernel_spmd(build_launch3b().nc, launch3b_inputs(inp, r3a),
                               core_ids=list(range(NCORE))).results
    r4 = run_bass_kernel_spmd(build_launch4().nc, launch4_inputs(inp, r2, r3b),
                              core_ids=list(range(NCORE))).results
    out = np.zeros((2, 16384, D), np.float32)
    for c in range(NCORE):
        bb, sseg = c // 4, c % 4
        out[bb, sseg * T:(sseg + 1) * T] = r4[c]["out"]
    return out


if __name__ == "__main__":
    build_launch1()
    print("built ok")


# version 04
# speedup vs baseline: 1.0401x; 1.0401x over previous
import numpy as np
import ml_dtypes
import concourse.bass as bass
import concourse.mybir as mybir
from concourse.bass_utils import run_bass_kernel_spmd

F32 = mybir.dt.float32
BF16 = mybir.dt.bfloat16
AF = mybir.ActivationFunctionType
ALU = mybir.AluOpType
AX = mybir.AxisListType

import os
NCORE = 8
T = int(os.environ.get("BASS_KERNEL_T", "4096"))
D = 1024
NT = T // 128
SHIFT0 = 40.0
LN_EPS = 1e-5
L0_EPS = 1e-6
RMS_EPS = 1e-5


class Sem:
    def __init__(self, nc, name):
        self.h = nc.semaphore(name).__enter__()
        self.n = 0

    def inc(self, k=1):
        self.n += k
        return self.n


class B:
    def __init__(self):
        self.nc = bass.Bass("TRN2", target_bir_lowering=False, debug=False)

    def sem(self, name):
        return Sem(self.nc, name)

    def sb(self, name, shape, dtype):
        return self.nc.sbuf_tensor(name, shape, dtype).__enter__()

    def ps(self, name, shape, dtype=F32):
        return self.nc.psum_tensor(name, shape, dtype).__enter__()

    def din(self, name, shape, dtype):
        return self.nc.dram_tensor(name, shape, dtype, kind="ExternalInput")

    def dout(self, name, shape, dtype):
        return self.nc.dram_tensor(name, shape, dtype, kind="ExternalOutput")

    def dtmp(self, name, shape, dtype):
        return self.nc.dram_tensor(name, shape, dtype, kind="Internal")


def emit_ln_vector(vector, sc, vv, x_ap, out_ap, s_sq, s_st, sc0):
    vector.tensor_reduce(s_st[:, 0:1], x_ap, axis=AX.X, op=ALU.add)
    vector.wait_ge(sc.h, sc0 + 1)
    vector.tensor_reduce(s_st[:, 1:2], s_sq[:], axis=AX.X, op=ALU.add)
    vector.drain()
    vector.tensor_scalar(s_st[:, 2:3], s_st[:, 0:1], 1.0 / D, None, op0=ALU.mult)
    vector.tensor_scalar(s_st[:, 3:4], s_st[:, 1:2], 1.0 / D, None, op0=ALU.mult)
    vector.drain()
    vector.tensor_tensor(s_st[:, 4:5], s_st[:, 2:3], s_st[:, 2:3], op=ALU.mult)
    vector.drain()
    vector.tensor_tensor(s_st[:, 5:6], s_st[:, 3:4], s_st[:, 4:5], op=ALU.subtract)
    vector.drain()
    vector.tensor_scalar(s_st[:, 5:6], s_st[:, 5:6], LN_EPS, None,
                         op0=ALU.add).then_inc(vv.h, 1); vv.inc()
    vector.wait_ge(sc.h, sc0 + 2)
    vector.reciprocal(s_st[:, 7:8], s_st[:, 6:7])
    vector.drain()
    return vector.scalar_tensor_tensor(out_ap, x_ap, s_st[:, 2:3],
                                       s_st[:, 7:8].broadcast_to([128, D]),
                                       op0=ALU.subtract, op1=ALU.mult)


def emit_ln_scalar(scalar, sc, vv, x_ap, s_sq, s_st, vv0):
    scalar.square(s_sq[:], x_ap).then_inc(sc.h, 1); sc.inc()
    scalar.wait_ge(vv.h, vv0 + 1)
    scalar.activation(s_st[:, 6:7], s_st[:, 5:6],
                      AF.Sqrt).then_inc(sc.h, 1); sc.inc()


def build_launch1():
    b = B()
    nc = b.nc
    x = b.din("x", [T, D], F32).ap()
    wq_h = b.din("wq_h", [16, 64, D], F32).ap()
    wk_h = b.din("wk_h", [16, 64, D], F32).ap()
    w0 = b.din("w0", [16, 64, 64], F32).ap()
    bq2 = b.din("bq2", [64, 16], F32).ap()
    bk2 = b.din("bk2", [64, 16], F32).ap()
    wvT = b.din("wvT", [D, D], F32).ap()
    wkT = b.din("wkT", [D, D], F32).ap()
    bv_row = b.din("bv_row", [1, D], F32).ap()
    bk_row = b.din("bk_row", [1, D], F32).ap()
    ones_row = b.din("ones_row", [1, 128], F32).ap()
    ones_col = b.din("ones_col", [128, 1], F32).ap()

    xn_bf = b.dout("xn_bf", [T, D], BF16).ap()
    phi_bf = b.dout("phi_bf", [T, D], BF16).ap()
    stats_N = b.dout("stats_N", [16, 64, 64], F32).ap()
    stats_D = b.dout("stats_D", [16, 64], F32).ap()

    DBG = os.environ.get("DEV_DBG") == "1"
    if DBG:
        s_logits = b.dout("s_logits", [T, D], F32).ap()
        s_v = b.dout("s_v", [T, D], BF16).ap()
        dbg_wqeff = b.dout("dbg_wqeff", [128, 8 * D], BF16).ap()
        dbg_beffq = b.dout("dbg_beffq", [1, D], BF16).ap()
        dbg_xnfm = b.dout("dbg_xnfm", [128, 8 * 512], BF16).ap()
    else:
        s_logits = b.dtmp("s_logits", [T, D], F32).ap()
        s_v = b.dtmp("s_v", [T, D], BF16).ap()
    s_beff = b.dtmp("s_beff", [2, 16, 64], F32).ap()

    s_wvT = b.sb("s_wvT", [128, 8, D], BF16)
    s_wkT = b.sb("s_wkT", [128, 8, D], BF16)
    s_whst = [b.sb("s_whst0", [64, 8, 128], BF16), b.sb("s_whst1", [64, 8, 128], BF16)]
    s_w0 = b.sb("s_w0", [64, 16, 64], BF16)
    s_bq2 = b.sb("s_bq2", [64, 16], BF16)
    s_bk2 = b.sb("s_bk2", [64, 16], BF16)
    s_bvrow = b.sb("s_bvrow", [1, D], BF16)
    s_bkrow = b.sb("s_bkrow", [1, D], BF16)
    s_ones = b.sb("s_ones", [1, 128], BF16)
    s_ones128 = b.sb("s_ones128", [128, 1], BF16)
    s_wqeff = b.sb("s_wqeff", [128, 8, D], BF16)
    s_wkeff = b.sb("s_wkeff", [128, 8, D], BF16)
    s_beffq = b.sb("s_beffq", [1, D], BF16)
    s_beffk = b.sb("s_beffk", [1, D], BF16)
    s_beff64 = b.sb("s_beff64", [64, 16], F32)

    s_x = b.sb("s_x", [128, D], F32)
    s_sq = b.sb("s_sq", [128, D], F32)
    s_st = b.sb("s_st", [128, 8], F32)
    s_xnbf = b.sb("s_xnbf", [128, D], BF16)
    s_xG = b.sb("s_xG", [128, 4, D], F32)
    s_sqG = b.sb("s_sqG", [128, 4, D], F32)
    s_xnG = b.sb("s_xnG", [128, 4, D], BF16)
    s_stG = b.sb("s_stG", [128, 4, 4], F32)

    s_xnfm = b.sb("s_xnfm", [128, 8, 512], BF16)
    s_ss = b.sb("s_ss", [128, 16], F32)
    s_lt = b.sb("s_lt", [128, D], F32)
    s_pt = b.sb("s_pt", [128, D], F32)
    s_pe = b.sb("s_pe", [128, D], F32)
    s_ph = b.sb("s_ph", [128, 16], F32)
    s_ph2 = b.sb("s_ph2", [128, 16], F32)
    s_vt = b.sb("s_vt", [128, D], BF16)
    s_phbf = b.sb("s_phbf", [128, D], BF16)
    s_at = b.sb("s_at", [128, D], BF16)
    s_lg2 = b.sb("s_lg2", [128, D], F32)
    s_vt2 = b.sb("s_vt2", [128, D], BF16)
    s_Nout = b.sb("s_Nout", [64, 1024], F32)
    s_Dout = b.sb("s_Dout", [64, 16], F32)

    ps_a = b.ps("ps_a", [128, 512])
    ps_b = b.ps("ps_b", [128, 512])
    ps_c = b.ps("ps_c", [128, 512])
    ps_d = b.ps("ps_d", [128, 512])
    ps_N1 = b.ps("ps_N1", [64, 512])
    ps_N2 = b.ps("ps_N2", [64, 512])
    ps_D = b.ps("ps_D", [64, 16])

    ld = b.sem("ld")
    lg = b.sem("lg")
    ssy = b.sem("ssy")
    sg = b.sem("sg")
    tv = b.sem("tv")
    vv = b.sem("vv")
    sc = b.sem("sc")

    with nc.Block() as k:
        @k.gpsimd
        def _(g):
            for dst, src in ((s_w0, w0.transpose([1, 0, 2])), (s_bq2, bq2),
                             (s_bk2, bk2), (s_bvrow, bv_row), (s_bkrow, bk_row),
                             (s_ones, ones_row), (s_ones128, ones_col)):
                g.dma_start(dst[:], src).then_inc(lg.h, 16); lg.inc(16)
            g.dma_start(s_wvT[:], wvT.rearrange("(c p) j -> p c j", p=128)).then_inc(lg.h, 16); lg.inc(16)
            g.dma_start(s_wkT[:], wkT.rearrange("(c p) j -> p c j", p=128)).then_inc(lg.h, 16); lg.inc(16)
            g.wait_ge(lg.h, lg.n)

    banks = [ps_a, ps_b, ps_c, ps_d]
    for wi in range(2):
        srcd = wq_h if wi == 0 else wk_h
        dst = s_wqeff if wi == 0 else s_wkeff
        for h in range(16):
            with nc.Block() as k:
                lg_tot = lg.n + 16

                @k.gpsimd
                def _(g, wi=wi, h=h, srcd=srcd, lg_tot=lg_tot):
                    g.dma_start(s_whst[h % 2][:],
                                srcd[h].rearrange("s (c p) -> s c p", p=128)).then_inc(lg.h, 16)
                    g.wait_ge(lg.h, lg_tot)
                lg.inc(16)

                @k.tensor
                def _(tensor, wi=wi, h=h, lg_tot=lg_tot):
                    tensor.wait_ge(lg.h, lg_tot)
                    pb = banks[h % 4]
                    for c in range(8):
                        tensor.matmul(pb[:, c * 64:(c + 1) * 64],
                                      s_whst[h % 2][:, c, :], s_w0[:, h, :],
                                      start=(c == 0), stop=(c == 7),
                                      skip_group_check=True)
            with nc.Block() as k:
                @k.vector
                def _(vector, wi=wi, h=h, dst=dst):
                    pb = banks[h % 4]
                    for c in range(8):
                        vector.tensor_copy(dst[:, c, h * 64:(h + 1) * 64],
                                           pb[:, c * 64:(c + 1) * 64])
    for wi in range(2):
        bcol = s_bq2 if wi == 0 else s_bk2
        with nc.Block() as k:
            @k.tensor
            def _(tensor, bcol=bcol):
                for h in range(16):
                    tensor.matmul(ps_D[:, h:h + 1], s_w0[:, h, :], bcol[:, h:h + 1],
                                  start=(h == 0), stop=(h == 15),
                                  skip_group_check=True)
        with nc.Block() as k:
            @k.vector
            def _(vector):
                vector.tensor_copy(s_beff64[:], ps_D[:])
        with nc.Block() as k:
            @k.sync
            def _(sync, wi=wi):
                sync.dma_start(s_beff[wi].transpose([1, 0]), s_beff64[:],
                               allow_slow_non_contiguous=True,
                               ).then_inc(ssy.h, 16); ssy.inc(16)
                sync.wait_ge(ssy.h, ssy.n)
    with nc.Block() as k:
        @k.gpsimd
        def _(g):
            g.dma_start(s_beffq[:], s_beff[0].rearrange("h f -> (h f)").unsqueeze(0)).then_inc(lg.h, 16); lg.inc(16)
            g.dma_start(s_beffk[:], s_beff[1].rearrange("h f -> (h f)").unsqueeze(0)).then_inc(lg.h, 16); lg.inc(16)
            g.wait_ge(lg.h, lg.n)

    for gi in range(T // 512):
        with nc.Block() as k:
            ld_tot = ld.n + 16
            sc0 = sc.n
            sg_before = sg.n

            @k.sync
            def _(sync, gi=gi, ld_tot=ld_tot):
                sync.dma_start(s_xG[:], x[gi * 512:(gi + 1) * 512, :]
                               .rearrange("(t p) d -> p t d", p=128)).then_inc(ld.h, 16)
                sync.wait_ge(ld.h, ld_tot)
            ld.inc(16)

            @k.vector
            def _(vector, ld_tot=ld_tot, sc0=sc0, sg_before=sg_before):
                vector.wait_ge(ld.h, ld_tot)
                vector.wait_ge(sg.h, sg_before)
                vector.tensor_reduce(s_stG[:, 0, :].unsqueeze(2), s_xG.ap(),
                                     axis=AX.X, op=ALU.add)
                vector.wait_ge(sc.h, sc0 + 1)
                vector.tensor_reduce(s_stG[:, 1, :].unsqueeze(2), s_sqG.ap(),
                                     axis=AX.X, op=ALU.add)
                vector.drain()
                vector.tensor_scalar(s_stG[:, 0:2, :], s_stG[:, 0:2, :], 1.0 / D,
                                     None, op0=ALU.mult)
                vector.drain()
                vector.tensor_tensor(s_stG[:, 2, :], s_stG[:, 0, :], s_stG[:, 0, :],
                                     op=ALU.mult)
                vector.drain()
                vector.tensor_tensor(s_stG[:, 2, :], s_stG[:, 1, :], s_stG[:, 2, :],
                                     op=ALU.subtract)
                vector.drain()
                vector.tensor_scalar(s_stG[:, 2, :], s_stG[:, 2, :], LN_EPS, None,
                                     op0=ALU.add).then_inc(vv.h, 1); vv.inc()
                vector.wait_ge(sc.h, sc0 + 2)
                vector.reciprocal(s_stG[:, 3, :], s_stG[:, 2, :])
                vector.drain()
                vector.tensor_tensor(s_sqG.ap(), s_xG.ap(),
                                     s_stG[:, 0, :].unsqueeze(2).broadcast_to([128, 4, D]),
                                     op=ALU.subtract)
                vector.drain()
                vector.tensor_tensor(s_xnG.ap(), s_sqG.ap(),
                                     s_stG[:, 3, :].unsqueeze(2).broadcast_to([128, 4, D]),
                                     op=ALU.mult).then_inc(vv.h, 1); vv.inc()

            @k.scalar
            def _(scalar, ld_tot=ld_tot):
                scalar.wait_ge(ld.h, ld_tot)
                scalar.square(s_sqG.ap().rearrange("p t d -> p (t d)"),
                              s_xG.ap().rearrange("p t d -> p (t d)")).then_inc(sc.h, 1); sc.inc()
                scalar.wait_ge(vv.h, vv.n - 1)
                scalar.activation(s_stG[:, 2, :], s_stG[:, 2, :],
                                  AF.Sqrt).then_inc(sc.h, 1); sc.inc()

            @k.gpsimd
            def _(g, gi=gi):
                g.wait_ge(vv.h, vv.n)
                g.dma_start(xn_bf[gi * 512:(gi + 1) * 512, :]
                            .rearrange("(t p) d -> p t d", p=128),
                            s_xnG[:]).then_inc(sg.h, 16)
            sg.inc(16)
    with nc.Block() as k:
        @k.gpsimd
        def _(g):
            g.wait_ge(sg.h, sg.n)

    NG = T // 512
    def proj_tile(tensor, xg, t, rhs, bias_row, pdst0, pdst1, first_starts):
        for jh, pdst in ((0, pdst0), (1, pdst1)):
            for kc in range(8):
                tensor.matmul(pdst, s_xnfm[:, kc, t * 128:(t + 1) * 128],
                              rhs[:, kc, jh * 512:(jh + 1) * 512],
                              start=(kc == 0), stop=False, skip_group_check=True)
            tensor.matmul(pdst, s_ones[:, 0:128], bias_row[:, jh * 512:(jh + 1) * 512],
                          start=False, stop=True, skip_group_check=True)

    for g in range(NG):
        with nc.Block() as k:
            @k.sync
            def _(sync, g=g):
                for kc in range(8):
                    sync.dma_start_transpose(
                        s_xnfm[:, kc, :],
                        xn_bf[g * 512:(g + 1) * 512, kc * 128:(kc + 1) * 128],
                    ).then_inc(ld.h, 16); ld.inc(16)
                sync.wait_ge(ld.h, ld.n)
        for t in range(4):
            i = g * 4 + t
            with nc.Block() as k:
                @k.tensor
                def _(tensor, g=g, t=t):
                    proj_tile(tensor, g, t, s_wvT, s_bvrow, ps_a[:], ps_b[:], True)
                    proj_tile(tensor, g, t, s_wkT, s_bkrow, ps_c[:], ps_d[:], True)
            with nc.Block() as k:
                @k.vector
                def _(vector):
                    vector.tensor_copy(s_vt[:, 0:512], ps_a[:])
                    vector.tensor_copy(s_vt[:, 512:1024], ps_b[:])

                @k.scalar
                def _(scalar):
                    scalar.square(s_sq[:, 0:512], ps_c[:])
                    scalar.square(s_sq[:, 512:1024], ps_d[:])
            with nc.Block() as k:
                @k.vector
                def _(vector):
                    vector.tensor_reduce(
                        s_ss.ap().rearrange("p (h o) -> p h o", o=1),
                        s_sq.ap().rearrange("p (h f) -> p h f", h=16),
                        axis=AX.X, op=ALU.add)
                    vector.drain()
                    vector.tensor_scalar(s_ss[:], s_ss[:], 0.5, None, op0=ALU.mult)
            with nc.Block() as k:
                @k.tensor
                def _(tensor, g=g, t=t):
                    proj_tile(tensor, g, t, s_wkeff, s_beffk, ps_a[:], ps_b[:], True)
                    proj_tile(tensor, g, t, s_wqeff, s_beffq, ps_c[:], ps_d[:], True)
            with nc.Block() as k:
                @k.vector
                def _(vector):
                    for jh, pb in ((0, ps_a), (1, ps_b)):
                        vector.scalar_tensor_tensor(
                            s_lt.ap()[:, jh * 512:(jh + 1) * 512].rearrange("p (h f) -> p h f", h=8),
                            s_ss.ap()[:, jh * 8:(jh + 1) * 8].unsqueeze(2).broadcast_to([128, 8, 64]),
                            -1.0,
                            pb.ap().rearrange("p (h f) -> p h f", h=8),
                            op0=ALU.mult, op1=ALU.add)
                    for jh, pb in ((0, ps_c), (1, ps_d)):
                        vector.tensor_reduce(
                            s_ph.ap()[:, jh * 8:(jh + 1) * 8].unsqueeze(2),
                            pb.ap().rearrange("p (h f) -> p h f", h=8),
                            axis=AX.X, op=ALU.max, negate=True)
                    vector.drain()
                    vector.tensor_tensor(
                        s_pt.ap()[:, 0:512].rearrange("p (h f) -> p h f", h=8),
                        ps_c.ap().rearrange("p (h f) -> p h f", h=8),
                        s_ph.ap()[:, 0:8].unsqueeze(2).broadcast_to([128, 8, 64]),
                        op=ALU.add)
                    vector.tensor_tensor(
                        s_pt.ap()[:, 512:1024].rearrange("p (h f) -> p h f", h=8),
                        ps_d.ap().rearrange("p (h f) -> p h f", h=8),
                        s_ph.ap()[:, 8:16].unsqueeze(2).broadcast_to([128, 8, 64]),
                        op=ALU.add).then_inc(vv.h, 1); vv.inc()

                @k.scalar
                def _(scalar):
                    scalar.wait_ge(vv.h, vv.n)
                    scalar.activation(s_pe[:], s_pt[:], AF.Exp).then_inc(sc.h, 1); sc.inc()

                @k.vector
                def _(vector):
                    vector.wait_ge(sc.h, sc.n)
                    vector.tensor_reduce(
                        s_ph.ap().rearrange("p (h o) -> p h o", o=1),
                        s_pe.ap().rearrange("p (h f) -> p h f", h=16),
                        axis=AX.X, op=ALU.add)
                    vector.drain()
                    vector.reciprocal(s_ph2[:], s_ph[:])
                    vector.drain()
                    vector.tensor_tensor(
                        s_phbf.ap().rearrange("p (h f) -> p h f", h=16),
                        s_pe.ap().rearrange("p (h f) -> p h f", h=16),
                        s_ph2.ap().unsqueeze(2).broadcast_to([128, 16, 64]),
                        op=ALU.mult)
            with nc.Block() as k:
                @k.gpsimd
                def _(g_, i=i):
                    g_.dma_start(s_v[i * 128:(i + 1) * 128, :], s_vt[:]).then_inc(sg.h, 16); sg.inc(16)
                    g_.dma_start(s_logits[i * 128:(i + 1) * 128, :], s_lt[:]).then_inc(sg.h, 16); sg.inc(16)
                    g_.dma_start(phi_bf[i * 128:(i + 1) * 128, :], s_phbf[:]).then_inc(sg.h, 16); sg.inc(16)
                    g_.wait_ge(sg.h, sg.n)

    for i in range(NT):
        with nc.Block() as k:
            ld_tot = ld.n + 32

            @k.sync
            def _(sync, i=i, ld_tot=ld_tot):
                sync.dma_start(s_lg2[:], s_logits[i * 128:(i + 1) * 128, :]).then_inc(ld.h, 16)
                sync.dma_start(s_vt2[:], s_v[i * 128:(i + 1) * 128, :]).then_inc(ld.h, 16)
                sync.wait_ge(ld.h, ld_tot)
            ld.inc(32)

            @k.vector
            def _(vector, ld_tot=ld_tot):
                vector.wait_ge(ld.h, ld_tot)
                vector.tensor_scalar(s_lg2[:], s_lg2[:], -SHIFT0, None,
                                     op0=ALU.add).then_inc(vv.h, 1); vv.inc()

            @k.scalar
            def _(scalar):
                scalar.wait_ge(vv.h, vv.n)
                scalar.activation(s_at[:], s_lg2[:], AF.Exp).then_inc(sc.h, 1); sc.inc()

            @k.tensor
            def _(tensor, i=i):
                tensor.wait_ge(sc.h, sc.n)
                for h in range(16):
                    pb = ps_N1 if h < 8 else ps_N2
                    cc = (h % 8) * 64
                    tensor.matmul(pb[:, cc:cc + 64],
                                  s_at[:, h * 64:(h + 1) * 64],
                                  s_vt2[:, h * 64:(h + 1) * 64],
                                  start=(i == 0 and h % 8 == 0), stop=(i == NT - 1),
                                  skip_group_check=True)
                for h in range(16):
                    tensor.matmul(ps_D[:, h:h + 1],
                                  s_at[:, h * 64:(h + 1) * 64],
                                  s_ones128[:],
                                  start=(i == 0 and h == 0), stop=(i == NT - 1),
                                  skip_group_check=True)
    with nc.Block() as k:
        @k.vector
        def _(vector):
            vector.tensor_copy(s_Nout[:, 0:512], ps_N1[:])
            vector.tensor_copy(s_Nout[:, 512:1024], ps_N2[:])
            vector.tensor_copy(s_Dout[:], ps_D[:])
    if DBG:
        with nc.Block() as k:
            @k.sync
            def _(sync):
                sync.dma_start(dbg_wqeff, s_wqeff.ap().rearrange("p c j -> p (c j)")).then_inc(ssy.h, 16); ssy.inc(16)
                sync.dma_start(dbg_beffq, s_beffq[:]).then_inc(ssy.h, 16); ssy.inc(16)
                sync.dma_start(dbg_xnfm, s_xnfm.ap().rearrange("p c j -> p (c j)")).then_inc(ssy.h, 16); ssy.inc(16)
                sync.wait_ge(ssy.h, ssy.n)
    with nc.Block() as k:
        @k.sync
        def _(sync):
            sync.dma_start(stats_N.transpose([1, 0, 2]),
                           s_Nout.ap().rearrange("f (h d) -> f h d", h=16)).then_inc(ssy.h, 16); ssy.inc(16)
            with nc.allow_non_contiguous_dma(reason="tiny 1k-elem transpose store"):
                sync.dma_start(stats_D.transpose([1, 0]), s_Dout[:]).then_inc(ssy.h, 16); ssy.inc(16)
            sync.wait_ge(ssy.h, ssy.n)
    return b


def build_launch2():
    PH = int(os.environ.get("BASS_L2_PHASES", "9"))
    b = B()
    nc = b.nc
    NGR = T // 512
    x = b.din("x", [T, D], F32).ap()
    xn_bf = b.din("xn_bf", [T, D], BF16).ap()
    phi_bf = b.din("phi_bf", [T, D], BF16).ap()
    statsN4 = b.din("statsN4", [4, 16, 64, 64], F32).ap()
    statsD4 = b.din("statsD4", [4, 16, 64], F32).ap()
    woT = b.din("woT", [D, D], F32).ap()
    bo_row = b.din("bo_row", [1, D], F32).ap()
    centT = b.din("centT", [D, 16], F32).ap()
    eb_row = b.din("eb_row", [1, 16], F32).ap()
    wrT = b.din("wrT", [16, D, D], F32).ap()
    br_e = b.din("br_e", [16, D], F32).ap()
    wsT = b.din("wsT", [2, D, D], F32).ap()
    bs2 = b.din("bs2", [1, 2 * D], F32).ap()
    selp = b.din("selp", [8, 16, 128], F32).ap()
    ones_row = b.din("ones_row", [1, 128], F32).ap()
    ones_col = b.din("ones_col", [128, 1], F32).ap()
    ident = b.din("ident", [128, 128], F32).ap()

    x1 = b.dout("x1", [T, D], F32).ap()
    xn1_bf = b.dout("xn1_bf", [T, D], BF16).ap()

    dN = b.dtmp("dN", [16, 64, 64], F32).ap()
    dD = b.dtmp("dD", [16, 64], F32).ap()
    dSeps = b.dtmp("dSeps", [1, 16], F32).ap()

    s_woT = b.sb("s_woT", [128, 8, D], BF16)
    s_wsT2 = b.sb("s_wsT2", [128, 16, D], BF16)
    s_wr = [b.sb("s_wr0", [128, 8, D], BF16), b.sb("s_wr1", [128, 8, D], BF16)]
    s_centT = b.sb("s_centT", [128, 8, 16], BF16)
    s_selp = b.sb("s_selp", [16, 8, 128], BF16)
    s_ident = b.sb("s_ident", [128, 128], F32)
    s_ones = b.sb("s_ones", [1, 128], BF16)
    s_onescol = b.sb("s_onescol", [128, 1], BF16)
    s_borow = b.sb("s_borow", [1, D], BF16)
    s_ebrow = b.sb("s_ebrow", [1, 16], BF16)
    s_bre = [b.sb("s_bre0", [1, D], BF16), b.sb("s_bre1", [1, D], BF16)]
    s_bs2 = b.sb("s_bs2", [1, 2 * D], F32)
    s_bsm = b.sb("s_bsm", [1, D], BF16)
    s_Nld = b.sb("s_Nld", [64, 1024], F32)
    s_Dld = b.sb("s_Dld", [64, 16], F32)
    s_Ncmb = b.sb("s_Ncmb", [64, 1024], F32)
    s_Dcmb = b.sb("s_Dcmb", [64, 16], F32)
    s_Dcmb_bf = b.sb("s_Dcmb_bf", [64, 16], BF16)
    s_seps = b.sb("s_seps", [1, 16], F32)
    s_Npair = b.sb("s_Npair", [128, 8, 128], BF16)
    s_Dbig = b.sb("s_Dbig", [128, D], F32)
    s_SepsB = b.sb("s_SepsB", [128, 16], F32)

    s_xnfm = b.sb("s_xnfm2", [128, 8, 512], BF16)
    s_phifm = b.sb("s_phifm", [128, 8, 512], BF16)
    s_xt = b.sb("s_xt", [128, D], F32)
    s_xnt = b.sb("s_xnt", [128, D], BF16)
    s_phit = b.sb("s_phit", [128, D], BF16)
    s_attn = b.sb("s_attn", [128, 4, D], F32)
    s_acc = b.sb("s_acc", [128, 4, D], F32)
    s_wcol = b.sb("s_wcol", [128, 4, 16], F32)

    s_p32 = b.sb("s_p32", [128, D], F32)
    s_den = b.sb("s_den", [128, 16], F32)
    s_rden = b.sb("s_rden", [128, 16], F32)
    s_rdT = b.sb("s_rdT", [16, 128], F32)
    s_rdT_bf = b.sb("s_rdT_bf", [16, 128], BF16)
    s_dex = b.sb("s_dex", [128, 128], F32)
    s_ctx = b.sb("s_ctx", [128, 8, 128], BF16)
    s_lg = b.sb("s_lg", [128, 16], F32)
    s_m8 = b.sb("s_m8", [128, 8], F32)
    s_m1n = b.sb("s_m1n", [128, 1], F32)
    s_er = b.sb("s_er", [128, 16], F32)
    s_sum = b.sb("s_sum", [128, 1], F32)
    s_rs = b.sb("s_rs", [128, 1], F32)
    s_msk = b.sb("s_msk", [128, 16], F32)
    s_x1t = b.sb("s_x1t", [128, D], F32)
    s_sq = b.sb("s_sq2", [128, D], F32)
    s_st = b.sb("s_st2", [128, 8], F32)
    s_xn1t = b.sb("s_xn1t", [128, D], BF16)

    ps_a = b.ps("ps2_a", [128, 512])
    ps_b = b.ps("ps2_b", [128, 512])
    ps_e = b.ps("ps2_e", [128, 128])
    ps_n = b.ps("ps2_n", [128, 128])
    ps_t = b.ps("ps2_t", [16, 128])
    ps_r = b.ps("ps2_r", [128, 16])
    ps_s = b.ps("ps2_s", [1, 16])

    ld = b.sem("ld")
    lg = b.sem("lg")
    lw = b.sem("lw")
    ssy = b.sem("ssy")
    sg = b.sem("sg")
    tv = b.sem("tv")
    vv = b.sem("vv")
    sc = b.sem("sc")

    with nc.Block() as k:
        @k.sync
        def _(sync):
            sync.dma_start(s_ident[:], ident).then_inc(ld.h, 16); ld.inc(16)
            sync.dma_start(s_bs2[:], bs2).then_inc(ld.h, 16); ld.inc(16)
            sync.wait_ge(ld.h, ld.n)

        @k.gpsimd
        def _(g):
            g.dma_start(s_woT[:], woT.rearrange("(c p) j -> p c j", p=128)).then_inc(lg.h, 16); lg.inc(16)
            g.dma_start(s_centT[:], centT.rearrange("(c p) j -> p c j", p=128)).then_inc(lg.h, 16); lg.inc(16)
            g.dma_start(s_selp[:], selp.transpose([1, 0, 2])).then_inc(lg.h, 16); lg.inc(16)
            g.dma_start(s_ones[:], ones_row).then_inc(lg.h, 16); lg.inc(16)
            g.dma_start(s_onescol[:], ones_col).then_inc(lg.h, 16); lg.inc(16)
            g.dma_start(s_borow[:], bo_row).then_inc(lg.h, 16); lg.inc(16)
            g.dma_start(s_ebrow[:], eb_row).then_inc(lg.h, 16); lg.inc(16)
            g.dma_start(s_wsT2[:, 0:8, :], wsT[0].rearrange("(c p) j -> p c j", p=128)).then_inc(lg.h, 16); lg.inc(16)
            g.dma_start(s_wsT2[:, 8:16, :], wsT[1].rearrange("(c p) j -> p c j", p=128)).then_inc(lg.h, 16); lg.inc(16)
            g.wait_ge(lg.h, lg.n)

    for ci in range(4):
        with nc.Block() as k:
            @k.sync
            def _(sync, ci=ci):
                sync.dma_start(s_Nld.ap().rearrange("f (h d) -> f h d", h=16), statsN4[ci].transpose([1, 0, 2])).then_inc(ld.h, 16); ld.inc(16)
                with nc.allow_non_contiguous_dma(reason="tiny"):
                    sync.dma_start(s_Dld[:], statsD4[ci].transpose([1, 0])).then_inc(ld.h, 16); ld.inc(16)
                sync.wait_ge(ld.h, ld.n)
        with nc.Block() as k:
            @k.vector
            def _(vector, ci=ci):
                if ci == 0:
                    vector.tensor_copy(s_Ncmb[:], s_Nld[:])
                    vector.tensor_copy(s_Dcmb[:], s_Dld[:])
                else:
                    vector.tensor_tensor(s_Ncmb[:], s_Ncmb[:], s_Nld[:], op=ALU.add)
                    vector.tensor_tensor(s_Dcmb[:], s_Dcmb[:], s_Dld[:], op=ALU.add)
    with nc.Block() as k:
        @k.vector
        def _(vector):
            vector.tensor_copy(s_Dcmb_bf[:], s_Dcmb[:])
            vector.tensor_tensor(s_bsm[:], s_bs2[:, 0:D], s_bs2[:, D:2 * D], op=ALU.add)
    with nc.Block() as k:
        @k.tensor
        def _(tensor):
            tensor.matmul(ps_s[:], s_onescol[0:64, 0:1], s_Dcmb_bf[:],
                          start=True, stop=True, skip_group_check=True)
    with nc.Block() as k:
        @k.vector
        def _(vector):
            vector.tensor_scalar(s_seps[:], ps_s[:], L0_EPS, None, op0=ALU.mult)
    with nc.Block() as k:
        @k.sync
        def _(sync):
            sync.dma_start(dN.transpose([1, 0, 2]),
                           s_Ncmb.ap().rearrange("f (h d) -> f h d", h=16)).then_inc(ssy.h, 16); ssy.inc(16)
            with nc.allow_non_contiguous_dma(reason="tiny"):
                sync.dma_start(dD.transpose([1, 0]), s_Dcmb[:]).then_inc(ssy.h, 16); ssy.inc(16)
            sync.dma_start(dSeps, s_seps[:]).then_inc(ssy.h, 16); ssy.inc(16)
            sync.wait_ge(ssy.h, ssy.n)

        @k.vector
        def _(vector):
            vector.memset(s_Npair[:], 0.0)
    with nc.Block() as k:
        @k.sync
        def _(sync):
            sync.dma_start(s_Dbig[:], dD.rearrange("h f -> (h f)").unsqueeze(0)
                           .broadcast_to([128, D])).then_inc(ld.h, 16); ld.inc(16)
            sync.dma_start(s_SepsB[:], dSeps.broadcast_to([128, 16])).then_inc(ld.h, 16); ld.inc(16)
            sync.wait_ge(ld.h, ld.n)

        @k.gpsimd
        def _(g):
            for p in range(8):
                g.dma_start(s_Npair[0:64, p, 0:64], dN[2 * p]).then_inc(lg.h, 16); lg.inc(16)
                g.dma_start(s_Npair[64:128, p, 64:128], dN[2 * p + 1]).then_inc(lg.h, 16); lg.inc(16)
            g.wait_ge(lg.h, lg.n)

    wr_need = [0]
    wr_ready = [0, 0]
    sg_guard = [0]
    moe_tb = [0]
    moe_vb = [0]

    def moe_mm(tensor, ew, e):
        moe_tb[0] = tv.n
        vb = vv.n
        nkc = ew.shape[1]
        for t in range(4):
            if t > 0:
                tensor.wait_ge(vv.h, vb + t)
            for jh in range(2):
                pd = ps_a if jh == 0 else ps_b
                for kc in range(nkc):
                    tensor.matmul(pd[:], s_xnfm[:, kc % 8, t * 128:(t + 1) * 128],
                                  ew[:, kc, jh * 512:(jh + 1) * 512],
                                  start=(kc == 0), stop=False, skip_group_check=True)
                if e is not None:
                    brow = s_bre[e % 2][:, jh * 512:(jh + 1) * 512]
                else:
                    brow = s_bsm[:, jh * 512:(jh + 1) * 512]
                mm = tensor.matmul(pd[:], s_ones[:, 0:128], brow,
                                   start=False, stop=True, skip_group_check=True)
                if jh == 1:
                    mm.then_inc(tv.h, 1); tv.inc()

    def moe_acc(vector, e):
        for t in range(4):
            vector.wait_ge(tv.h, moe_tb[0] + t + 1)
            for jh in range(2):
                pd = ps_a if jh == 0 else ps_b
                dst = s_acc[:, t, jh * 512:(jh + 1) * 512]
                if e is not None:
                    op = vector.scalar_tensor_tensor(dst, pd[:], s_wcol[:, t, e:e + 1],
                                                     dst, op0=ALU.mult, op1=ALU.add)
                else:
                    op = vector.scalar_tensor_tensor(dst, pd[:], 0.5, dst,
                                                     op0=ALU.mult, op1=ALU.add)
                vector.drain()
                if jh == 1:
                    op.then_inc(vv.h, 1); vv.inc()

    for g in range(NGR):
        r0 = g * 512
        with nc.Block() as k:
            @k.sync
            def _(sync, g=g, r0=r0):
                for kc in range(8):
                    sync.dma_start_transpose(
                        s_xnfm[:, kc, :], xn_bf[r0:r0 + 512, kc * 128:(kc + 1) * 128]
                    ).then_inc(ld.h, 16); ld.inc(16)
                    sync.dma_start_transpose(
                        s_phifm[:, kc, :], phi_bf[r0:r0 + 512, kc * 128:(kc + 1) * 128]
                    ).then_inc(ld.h, 16); ld.inc(16)
                sync.wait_ge(ld.h, ld.n)

            if g == 0:
                @k.gpsimd
                def _(gp):
                    gp.dma_start(s_wr[0][:], wrT[0].rearrange("(c p) j -> p c j", p=128)).then_inc(lw.h, 16); lw.inc(16)
                    gp.dma_start(s_bre[0][:], br_e[0].unsqueeze(0)).then_inc(lw.h, 16)
                wr_need[0] = lw.inc(16)

        for t in range(4 if PH >= 2 else 0):
            with nc.Block() as k:
                @k.sync
                def _(sync, g=g, t=t):
                    r = g * 512 + t * 128
                    sync.dma_start(s_phit[:], phi_bf[r:r + 128, :]).then_inc(ld.h, 16); ld.inc(16)
                    sync.wait_ge(ld.h, ld.n)
            with nc.Block() as k:
                @k.vector
                def _(vector, t=t):
                    vector.tensor_copy(s_p32[:], s_phit[:])
                    vector.drain()
                    vector.tensor_tensor(s_p32[:], s_p32[:], s_Dbig[:], op=ALU.mult)
                    vector.drain()
                    vector.tensor_reduce(s_den.ap().unsqueeze(2),
                                         s_p32.ap().rearrange("p (h f) -> p h f", h=16),
                                         axis=AX.X, op=ALU.add)
                    vector.drain()
                    vector.tensor_tensor(s_den[:], s_den[:], s_SepsB[:], op=ALU.add)
                    vector.drain()
                    vector.reciprocal(s_rden[:], s_den[:])
            with nc.Block() as k:
                @k.tensor
                def _(tensor, t=t):
                    tensor.transpose(ps_t[:], s_rden[:], s_ident[:])
                    for kc in range(8):
                        tensor.matmul(ps_r[:], s_xnfm[:, kc, t * 128:(t + 1) * 128],
                                      s_centT[:, kc, :], start=(kc == 0), stop=False,
                                      skip_group_check=True)
                    tensor.matmul(ps_r[:], s_ones[:, 0:128], s_ebrow[:],
                                  start=False, stop=True, skip_group_check=True)
            with nc.Block() as k:
                @k.vector
                def _(vector, t=t):
                    vector.tensor_copy(s_rdT[:], ps_t[:])
                    vector.tensor_copy(s_lg[:], ps_r[:])
                    vector.drain()
                    vector.tensor_copy(s_rdT_bf[:], s_rdT[:])
                    vector.max(out=s_m8[:], in_=s_lg[:])
                    vector.drain()
                    vector.tensor_scalar(s_m1n[:], s_m8[:, 0:1], -1.0, None,
                                         op0=ALU.mult)
                    vector.tensor_tensor(s_msk[:], s_lg[:],
                                         s_m8[:, 1:2].broadcast_to([128, 16]),
                                         op=ALU.is_ge).then_inc(vv.h, 1); vv.inc()

                @k.scalar
                def _(scalar):
                    scalar.wait_ge(vv.h, vv.n)
                    scalar.activation(s_er[:], s_lg[:], AF.Exp, bias=s_m1n[:],
                                      accum_out=s_sum[:]).then_inc(sc.h, 1); sc.inc()

                @k.vector
                def _(vector, t=t):
                    vector.wait_ge(sc.h, sc.n)
                    vector.reciprocal(s_rs[:], s_sum[:])
                    vector.tensor_tensor(s_er[:], s_er[:], s_msk[:], op=ALU.mult)
                    vector.drain()
                    vector.tensor_scalar(s_wcol[:, t, :], s_er[:], s_rs[:], None,
                                         op0=ALU.mult)
            if PH >= 3:
                with nc.Block() as k:
                    tv0, sc0, vv0 = tv.n, sc.n, vv.n

                    @k.tensor
                    def _(tensor, t=t, tv0=tv0, sc0=sc0, vv0=vv0):
                        for p in range(8):
                            if p >= 1:
                                tensor.wait_ge(sc.h, sc0 + p)
                                tensor.wait_ge(vv.h, vv0 + p)
                            tensor.matmul(ps_e[:], s_selp[:, p, :], s_rdT_bf[:],
                                          start=True, stop=True, skip_group_check=True)
                            tensor.matmul(ps_n[:], s_Npair[:, p, :],
                                          s_phifm[:, p, t * 128:(t + 1) * 128],
                                          start=True, stop=True,
                                          skip_group_check=True).then_inc(tv.h, 1); tv.inc()
                        tensor.wait_ge(vv.h, vv0 + 8)
                        for jh in range(2):
                            pd = ps_a if jh == 0 else ps_b
                            for kc in range(8):
                                tensor.matmul(pd[:], s_ctx[:, kc, :],
                                              s_woT[:, kc, jh * 512:(jh + 1) * 512],
                                              start=(kc == 0), stop=False, skip_group_check=True)
                            tensor.matmul(pd[:], s_ones[:, 0:128],
                                          s_borow[:, jh * 512:(jh + 1) * 512],
                                          start=False, stop=True,
                                          skip_group_check=True).then_inc(tv.h, 1); tv.inc()

                    @k.scalar
                    def _(scalar, tv0=tv0, vv0=vv0):
                        for p in range(8):
                            scalar.wait_ge(tv.h, tv0 + p + 1)
                            if p >= 1:
                                scalar.wait_ge(vv.h, vv0 + p)
                            scalar.copy(s_dex[:], ps_e[:]).then_inc(sc.h, 1); sc.inc()

                    @k.vector
                    def _(vector, t=t, tv0=tv0, sc0=sc0):
                        for p in range(8):
                            vector.wait_ge(sc.h, sc0 + p + 1)
                            vector.tensor_tensor(s_ctx[:, p, :], ps_n[:], s_dex[:],
                                                 op=ALU.mult).then_inc(vv.h, 1); vv.inc()
                            vector.drain()
                        vector.wait_ge(tv.h, tv0 + 10)
                        vector.tensor_copy(s_attn[:, t, 0:512], ps_a[:])
                        vector.tensor_copy(s_attn[:, t, 512:1024], ps_b[:])

        with nc.Block() as k:
            @k.vector
            def _(vector):
                vector.memset(s_acc[:], 0.0)
        for e in range(16 if PH >= 4 else 0):
            with nc.Block() as k:
                @k.gpsimd
                def _(gp, g=g, e=e):
                    gp.wait_ge(lw.h, lw.n)
                    if e < 15:
                        gp.dma_start(s_wr[(e + 1) % 2][:],
                                     wrT[e + 1].rearrange("(c p) j -> p c j", p=128)
                                     ).then_inc(lw.h, 16); lw.inc(16)
                        gp.dma_start(s_bre[(e + 1) % 2][:],
                                     br_e[e + 1].unsqueeze(0)).then_inc(lw.h, 16)
                        wr_ready[(e + 1) % 2] = lw.inc(16)
                    elif g < NGR - 1:
                        gp.dma_start(s_wr[0][:],
                                     wrT[0].rearrange("(c p) j -> p c j", p=128)
                                     ).then_inc(lw.h, 16); lw.inc(16)
                        gp.dma_start(s_bre[0][:],
                                     br_e[0].unsqueeze(0)).then_inc(lw.h, 16)
                        wr_ready[0] = lw.inc(16)

                @k.tensor
                def _(tensor, e=e):
                    tensor.wait_ge(lw.h, wr_need[0])
                    moe_mm(tensor, s_wr[e % 2], e)

                @k.vector
                def _(vector, e=e):
                    moe_acc(vector, e)
            wr_need[0] = wr_ready[(e + 1) % 2]
        if PH >= 4:
            with nc.Block() as k:
                @k.tensor
                def _(tensor):
                    moe_mm(tensor, s_wsT2, None)

                @k.vector
                def _(vector):
                    moe_acc(vector, None)

        for t in range(4 if PH >= 5 else 0):
            with nc.Block() as k:
                sgn = sg.n

                @k.sync
                def _(sync, g=g, t=t):
                    r = g * 512 + t * 128
                    sync.dma_start(s_xt[:], x[r:r + 128, :]).then_inc(ld.h, 16); ld.inc(16)
                    sync.dma_start(s_xnt[:], xn_bf[r:r + 128, :]).then_inc(ld.h, 16); ld.inc(16)
                    sync.wait_ge(ld.h, ld.n)

                @k.vector
                def _(vector, t=t, sgn=sgn):
                    vector.wait_ge(ld.h, ld.n)
                    vector.wait_ge(sg.h, sgn)
                    vector.tensor_copy(s_x1t[:], s_xnt[:])
                    vector.drain()
                    vector.tensor_tensor(s_x1t[:], s_x1t[:], s_xt[:], op=ALU.add)
                    vector.drain()
                    vector.tensor_tensor(s_x1t[:], s_x1t[:], s_attn[:, t, :], op=ALU.add)
                    vector.drain()
                    vector.tensor_tensor(s_x1t[:], s_x1t[:], s_acc[:, t, :], op=ALU.add)
            with nc.Block() as k:
                sc0, vv0 = sc.n, vv.n

                @k.vector
                def _(vector, sc0=sc0):
                    emit_ln_vector(vector, sc, vv, s_x1t[:], s_xn1t[:], s_sq, s_st,
                                   sc0).then_inc(vv.h, 1)
                vv.inc()

                @k.scalar
                def _(scalar, vv0=vv0):
                    emit_ln_scalar(scalar, sc, vv, s_x1t[:], s_sq, s_st, vv0)

                @k.gpsimd
                def _(gp, g=g, t=t):
                    gp.wait_ge(vv.h, vv.n)
                    r = g * 512 + t * 128
                    gp.dma_start(x1[r:r + 128, :], s_x1t[:]).then_inc(sg.h, 16); sg.inc(16)
                    gp.dma_start(xn1_bf[r:r + 128, :], s_xn1t[:]).then_inc(sg.h, 16); sg.inc(16)
    with nc.Block() as k:
        @k.gpsimd
        def _(gp):
            gp.wait_ge(sg.h, sg.n)
    return b


def build_launch3a():
    b = B()
    nc = b.nc
    NGR = T // 512
    xn1_bf = b.din("xn1_bf", [T, D], BF16).ap()
    wq1T = b.din("wq1T", [D, D], F32).ap()
    wk1T = b.din("wk1T", [D, D], F32).ap()
    wv1T = b.din("wv1T", [D, D], F32).ap()
    wg1T = b.din("wg1T", [D, 128], F32).ap()
    wg2T = b.din("wg2T", [128, D], F32).ap()
    identf = b.din("identf", [128, 128], F32).ap()

    q_fm = b.dout("q_fm", [D, T], BF16).ap()
    w_fm = b.dout("w_fm", [D, T], F32).ap()
    E_fm = b.dout("E_fm", [D, T], F32).ap()
    v_tm = b.dout("v_tm", [T, D], BF16).ap()
    gate_tm = b.dout("gate_tm", [T, D], BF16).ap()
    Wc = b.dout("Wc", [8, 128], F32).ap()
    Clin = b.dout("Clin", [8, 128, 128], F32).ap()

    s_wq1T = b.sb("s_wq1T", [128, 8, D], BF16)
    s_wk1T = b.sb("s_wk1T", [128, 8, D], BF16)
    s_wv1T = b.sb("s_wv1T", [128, 8, D], BF16)
    s_wg1T = b.sb("s_wg1T", [128, 8, 128], BF16)
    s_wg2T = b.sb("s_wg2T", [128, D], BF16)
    s_idbf = b.sb("s_idbf", [128, 128], BF16)

    s_xnfm = b.sb("s3_xnfm", [128, 8, 512], BF16)
    s_qg = b.sb("s_qg", [128, 8, 512], BF16)
    s_wf = b.sb("s_wf", [128, 8, 512], F32)
    s_wb = b.sb("s_wb", [128, 8, 512], BF16)
    s_Ef = b.sb("s_Ef", [128, 8, 512], F32)
    s_Ecar = b.sb("s_Ecar", [128, 8], F32)
    s_vt = b.sb("s3_vt", [128, D], BF16)
    s_xgt = b.sb("s_xgt", [128, 128], BF16)
    s_xgT = b.sb("s_xgT", [128, 128], BF16)
    s_gt = b.sb("s_gt", [128, D], BF16)
    s_wtmB = b.sb("s_wtmB", [128, 8, 128], BF16)
    s_Cout = b.sb("s_Cout", [128, 8, 128], F32)

    ps_a = b.ps("ps3_a", [128, 512])
    ps_b = b.ps("ps3_b", [128, 512])
    ps_g = b.ps("ps3_g", [128, 128])
    ps_t = b.ps("ps3_t", [128, 128], BF16)
    ps_tw = b.ps("ps3_tw", [128, 1024], BF16)
    ps_C = b.ps("ps3_C", [128, 1024])

    ld = b.sem("ld")
    lg = b.sem("lg")
    ssy = b.sem("ssy")
    sg = b.sem("sg")
    tv = b.sem("tv")
    vv = b.sem("vv")
    sc = b.sem("sc")

    with nc.Block() as k:
        @k.gpsimd
        def _(g):
            for dst, src in ((s_wq1T, wq1T), (s_wk1T, wk1T), (s_wv1T, wv1T)):
                g.dma_start(dst[:], src.rearrange("(c p) j -> p c j", p=128)).then_inc(lg.h, 16); lg.inc(16)
            g.dma_start(s_wg1T[:], wg1T.rearrange("(c p) j -> p c j", p=128)).then_inc(lg.h, 16); lg.inc(16)
            g.dma_start(s_wg2T[:], wg2T).then_inc(lg.h, 16); lg.inc(16)
            g.dma_start(s_idbf[:], identf).then_inc(lg.h, 16); lg.inc(16)
            g.wait_ge(lg.h, lg.n)

        @k.vector
        def _(vector):
            vector.memset(s_Ecar[:], 0.0)

    for g in range(NGR):
        r0 = g * 512
        with nc.Block() as k:
            @k.sync
            def _(sync, g=g, r0=r0):
                for kc in range(8):
                    sync.dma_start_transpose(
                        s_xnfm[:, kc, :], xn1_bf[r0:r0 + 512, kc * 128:(kc + 1) * 128]
                    ).then_inc(ld.h, 16); ld.inc(16)
                sync.wait_ge(ld.h, ld.n)

        for which in range(2):
            src_w = s_wq1T if which == 0 else s_wk1T
            with nc.Block() as k:
                tv0, sc0 = tv.n, sc.n

                @k.tensor
                def _(tensor, which=which, src_w=src_w, tv0=tv0, sc0=sc0):
                    for j in range(8):
                        pd = ps_a if j % 2 == 0 else ps_b
                        if j >= 2:
                            tensor.wait_ge(sc.h, sc0 + j - 1)
                        for kc in range(8):
                            mm = tensor.matmul(pd[:], src_w[:, kc, j * 128:(j + 1) * 128],
                                               s_xnfm[:, kc, :], start=(kc == 0), stop=(kc == 7),
                                               skip_group_check=True)
                        mm.then_inc(tv.h, 1); tv.inc()

                @k.scalar
                def _(scalar, which=which, tv0=tv0):
                    for j in range(8):
                        pd = ps_a if j % 2 == 0 else ps_b
                        scalar.wait_ge(tv.h, tv0 + j + 1)
                        if which == 0:
                            scalar.activation(s_qg[:, j, :], pd[:], AF.Silu).then_inc(sc.h, 1); sc.inc()
                        else:
                            scalar.activation(s_wf[:, j, :], pd[:], AF.Exp).then_inc(sc.h, 1); sc.inc()
        with nc.Block() as k:
            @k.vector
            def _(vector):
                vector.tensor_copy(s_wb[:], s_wf[:])
                for kc in range(8):
                    vector.drain()
                    vector.tensor_tensor_scan(s_Ef[:, kc, :], s_wf[:, kc, :],
                                              s_wf[:, kc, :], s_Ecar[:, kc:kc + 1],
                                              op0=ALU.add, op1=ALU.bypass)
                    vector.drain()
                    vector.tensor_copy(s_Ecar[:, kc:kc + 1], s_Ef[:, kc, 511:512])

        with nc.Block() as k:
            @k.gpsimd
            def _(gp, g=g, r0=r0):
                gp.dma_start(q_fm.rearrange("(c p) t -> p c t", p=128)[:, :, r0:r0 + 512],
                             s_qg[:]).then_inc(sg.h, 16); sg.inc(16)
                gp.dma_start(w_fm.rearrange("(c p) t -> p c t", p=128)[:, :, r0:r0 + 512],
                             s_wf[:]).then_inc(sg.h, 16); sg.inc(16)
                gp.dma_start(E_fm.rearrange("(c p) t -> p c t", p=128)[:, :, r0:r0 + 512],
                             s_Ef[:]).then_inc(sg.h, 16); sg.inc(16)
                gp.wait_ge(sg.h, sg.n)

        for t in range(4):
            i = g * 4 + t
            with nc.Block() as k:
                @k.tensor
                def _(tensor, t=t):
                    for jh in range(2):
                        pd = ps_a if jh == 0 else ps_b
                        for kc in range(8):
                            tensor.matmul(pd[:], s_xnfm[:, kc, t * 128:(t + 1) * 128],
                                          s_wv1T[:, kc, jh * 512:(jh + 1) * 512],
                                          start=(kc == 0), stop=(kc == 7),
                                          skip_group_check=True)
                    for kc in range(8):
                        tensor.matmul(ps_g[:], s_xnfm[:, kc, t * 128:(t + 1) * 128],
                                      s_wg1T[:, kc, :], start=(kc == 0), stop=(kc == 7),
                                      skip_group_check=True)
            with nc.Block() as k:
                @k.vector
                def _(vector):
                    vector.tensor_copy(s_vt[:, 0:512], ps_a[:])
                    vector.tensor_copy(s_vt[:, 512:1024], ps_b[:])
                    vector.tensor_copy(s_xgt[:], ps_g[:])
            with nc.Block() as k:
                @k.tensor
                def _(tensor):
                    tensor.transpose(ps_t[:], s_xgt[:], s_idbf[:])
            with nc.Block() as k:
                @k.vector
                def _(vector):
                    vector.tensor_copy(s_xgT[:], ps_t[:])
            with nc.Block() as k:
                @k.tensor
                def _(tensor):
                    for jh in range(2):
                        pd = ps_a if jh == 0 else ps_b
                        tensor.matmul(pd[:], s_xgT[:], s_wg2T[:, jh * 512:(jh + 1) * 512],
                                      start=True, stop=True, skip_group_check=True)
            with nc.Block() as k:
                @k.scalar
                def _(scalar):
                    scalar.activation(s_gt[:, 0:512], ps_a[:], AF.Silu)
                    scalar.activation(s_gt[:, 512:1024], ps_b[:], AF.Silu)
            with nc.Block() as k:
                @k.tensor
                def _(tensor, t=t):
                    for h in range(8):
                        tensor.transpose(ps_tw[:, h * 128:(h + 1) * 128],
                                         s_wb[:, h, t * 128:(t + 1) * 128], s_idbf[:])
            with nc.Block() as k:
                @k.vector
                def _(vector):
                    vector.tensor_copy(s_wtmB.ap().rearrange("p h c -> p (h c)"), ps_tw[:])
            with nc.Block() as k:
                @k.tensor
                def _(tensor, i=i):
                    for h in range(8):
                        tensor.matmul(ps_C[:, h * 128:(h + 1) * 128], s_wtmB[:, h, :],
                                      s_vt[:, h * 128:(h + 1) * 128],
                                      start=(i == 0 and h % 4 == 0), stop=(i == NT - 1),
                                      skip_group_check=True)
            with nc.Block() as k:
                @k.gpsimd
                def _(gp, i=i):
                    gp.dma_start(v_tm[i * 128:(i + 1) * 128, :], s_vt[:]).then_inc(sg.h, 16); sg.inc(16)
                    gp.dma_start(gate_tm[i * 128:(i + 1) * 128, :], s_gt[:]).then_inc(sg.h, 16); sg.inc(16)
                    gp.wait_ge(sg.h, sg.n)

    with nc.Block() as k:
        @k.vector
        def _(vector):
            vector.tensor_copy(s_Cout.ap().rearrange("p h c -> p (h c)"), ps_C[:])
    with nc.Block() as k:
        @k.sync
        def _(sync):
            sync.dma_start(Clin.transpose([1, 0, 2]), s_Cout[:]).then_inc(ssy.h, 16); ssy.inc(16)
            with nc.allow_non_contiguous_dma(reason="tiny"):
                sync.dma_start(Wc.transpose([1, 0]), s_Ecar[:]).then_inc(ssy.h, 16); ssy.inc(16)
            sync.wait_ge(ssy.h, ssy.n)
    return b


def build_launch3b():
    b = B()
    nc = b.nc
    q_fm = b.din("q_fm", [D, T], BF16).ap()
    w_fm = b.din("w_fm", [D, T], F32).ap()
    E_fm = b.din("E_fm", [D, T], F32).ap()
    v_tm = b.din("v_tm", [T, D], BF16).ap()
    gate_tm = b.din("gate_tm", [T, D], BF16).ap()
    Wc4 = b.din("Wc4", [4, 8, 128], F32).ap()
    Clin4 = b.din("Clin4", [4, 8, 128, 128], F32).ap()
    mask4 = b.din("mask4", [1, 4], F32).ap()
    segflag = b.din("segflag", [1, 1], F32).ap()
    wo1T = b.din("wo1T", [D, D], F32).ap()
    gn_row = b.din("gn_row", [1, D], F32).ap()
    identf = b.din("identf", [128, 128], F32).ap()
    ut = b.din("ut", [128, 128], F32).ap()

    ya = b.dout("ya", [T, D], F32).ap()

    s_wo1T = b.sb("s_wo1T", [128, 8, D], BF16)
    s_idbf = b.sb("s3b_idbf", [128, 128], BF16)
    s_ut = b.sb("s_ut", [128, 128], BF16)
    s_gn = b.sb("s_gn", [128, D], F32)
    s_maskB = b.sb("s_maskB", [128, 4], F32)
    s_flag = b.sb("s_flag", [128, 1], F32)
    s_Wc4 = b.sb("s_Wc4", [128, 4, 8], F32)
    s_Ein = b.sb("s_Ein", [128, 8], F32)
    s_rEin = b.sb("s_rEin", [128, 8], F32)
    s_Cld = b.sb("s_Cld", [128, 8, 128], F32)
    s_Sf = b.sb("s_Sf", [128, 8, 128], F32)
    s_Sbf = b.sb("s_Sbf", [128, 8, 128], BF16)
    s_Eprev = b.sb("s_Eprev", [128, 8], F32)
    s_w0col = b.sb("s_w0col", [128, 8], F32)
    s_tmp8 = b.sb("s_tmp8", [128, 8], F32)

    s_qc = b.sb("s_qc", [128, 8, 128], BF16)
    s_wc = b.sb("s_wc", [128, 8, 128], F32)
    s_Ec = b.sb("s_Ec", [128, 8, 128], F32)
    s_vc = b.sb("s_vc", [128, D], BF16)
    s_gc = b.sb("s_gc", [128, D], BF16)
    s_rEgB = b.sb("s_rEgB", [128, 8, 128], F32)
    s_q2B = b.sb("s_q2B", [128, 8, 128], BF16)
    s_k2B = b.sb("s_k2B", [128, 8, 128], BF16)
    s_kTfB = b.sb("s_kTfB", [128, 8, 128], BF16)
    s_kTtB = b.sb("s_kTtB", [128, 8, 128], BF16)
    s_AbB = b.sb("s_AbB", [128, 8, 128], BF16)
    s_dec = b.sb("s_dec", [128, 8], F32)
    s_rEp = b.sb("s_rEp", [128, 8], F32)
    s_rEp2 = b.sb("s_rEp2", [128, 8], F32)
    s_t0B = b.sb("s_t0B", [128, 8, 128], F32)
    s_o = b.sb("s_o", [128, D], F32)
    s_osq = b.sb("s_osq", [128, D], F32)
    s_ost = b.sb("s_ost", [128, 8], F32)
    s_gc32 = b.sb("s_gc32", [128, D], F32)
    s_rgb = b.sb("s_rgb", [128, D], BF16)
    s_rgT = b.sb("s_rgT", [128, 8, 128], BF16)
    s_yat = b.sb("s_yat", [128, D], F32)

    ps_o = b.ps("ps3b_o", [128, 1024])
    ps_AY = b.ps("ps3b_AY", [128, 1024])
    ps_S = b.ps("ps3b_S", [128, 1024])
    ps_T = b.ps("ps3b_T", [128, 1024], BF16)

    ld = b.sem("ld")
    lg = b.sem("lg")
    sg = b.sem("sg")
    tv = b.sem("tv")
    vv = b.sem("vv")
    sc = b.sem("sc")

    with nc.Block() as k:
        @k.sync
        def _(sync):
            sync.dma_start(s_gn[:], gn_row.broadcast_to([128, D])).then_inc(ld.h, 16); ld.inc(16)
            sync.dma_start(s_maskB[:], mask4.broadcast_to([128, 4])).then_inc(ld.h, 16); ld.inc(16)
            sync.dma_start(s_flag[:], segflag.broadcast_to([128, 1])).then_inc(ld.h, 16); ld.inc(16)
            sync.dma_start(s_Wc4[:], Wc4.transpose([2, 0, 1]), allow_slow_non_contiguous=True).then_inc(ld.h, 16); ld.inc(16)
            with nc.allow_non_contiguous_dma(reason="tiny w0col"):
                sync.dma_start(s_w0col[:], w_fm.rearrange("(c p) t -> p t c", p=128)[:, 0:1, :].squeeze(1)).then_inc(ld.h, 16); ld.inc(16)
            sync.wait_ge(ld.h, ld.n)

        @k.gpsimd
        def _(g):
            g.dma_start(s_wo1T[:], wo1T.rearrange("(c p) j -> p c j", p=128)).then_inc(lg.h, 16); lg.inc(16)
            g.dma_start(s_idbf[:], identf).then_inc(lg.h, 16); lg.inc(16)
            g.dma_start(s_ut[:], ut).then_inc(lg.h, 16); lg.inc(16)
            g.wait_ge(lg.h, lg.n)
    with nc.Block() as k:
        @k.vector
        def _(vector):
            vector.tensor_tensor(s_Wc4.ap(), s_Wc4.ap(),
                                 s_maskB.ap().unsqueeze(2).broadcast_to([128, 4, 8]),
                                 op=ALU.mult)
            vector.drain()
            vector.tensor_reduce(s_Ein.ap().unsqueeze(2),
                                 s_Wc4.ap().transpose([0, 2, 1]), axis=AX.X, op=ALU.add)
            vector.drain()
            vector.tensor_tensor(s_tmp8[:], s_Ein[:],
                                 s_flag.ap().broadcast_to([128, 8]), op=ALU.add)
            vector.drain()
            vector.reciprocal(s_rEin[:], s_tmp8[:])
            vector.drain()
            vector.tensor_tensor(s_tmp8[:], s_w0col[:],
                                 s_flag.ap().broadcast_to([128, 8]), op=ALU.mult)
            vector.drain()
            vector.tensor_tensor(s_Eprev[:], s_Ein[:], s_tmp8[:], op=ALU.add)
            vector.memset(s_Sf[:], 0.0)
    for c4 in range(4):
        with nc.Block() as k:
            @k.sync
            def _(sync, c4=c4):
                sync.dma_start(s_Cld[:], Clin4[c4].transpose([1, 0, 2])).then_inc(ld.h, 16); ld.inc(16)
                sync.wait_ge(ld.h, ld.n)
        with nc.Block() as k:
            @k.vector
            def _(vector, c4=c4):
                vector.scalar_tensor_tensor(s_Sf.ap().rearrange("p h c -> p (h c)"),
                                            s_Cld.ap().rearrange("p h c -> p (h c)"),
                                            s_maskB[:, c4:c4 + 1],
                                            s_Sf.ap().rearrange("p h c -> p (h c)"),
                                            op0=ALU.mult, op1=ALU.add)
                vector.drain()
    with nc.Block() as k:
        @k.vector
        def _(vector):
            for h in range(8):
                vector.tensor_scalar(s_Sf[:, h, :], s_Sf[:, h, :],
                                     s_rEin[:, h:h + 1], None, op0=ALU.mult)
                vector.drain()
            vector.tensor_copy(s_Sbf[:], s_Sf[:])

    for i in range(NT):
        r = i * 128
        with nc.Block() as k:
            @k.sync
            def _(sync, r=r):
                sync.dma_start(s_qc[:], q_fm.rearrange("(c p) t -> p c t", p=128)[:, :, r:r + 128]).then_inc(ld.h, 16); ld.inc(16)
                sync.dma_start(s_wc[:], w_fm.rearrange("(c p) t -> p c t", p=128)[:, :, r:r + 128]).then_inc(ld.h, 16); ld.inc(16)
                sync.dma_start(s_Ec[:], E_fm.rearrange("(c p) t -> p c t", p=128)[:, :, r:r + 128]).then_inc(ld.h, 16); ld.inc(16)
                sync.dma_start(s_vc[:], v_tm[r:r + 128, :]).then_inc(ld.h, 16); ld.inc(16)
                sync.dma_start(s_gc[:], gate_tm[r:r + 128, :]).then_inc(ld.h, 16); ld.inc(16)
                sync.wait_ge(ld.h, ld.n)
        with nc.Block() as k:
            @k.vector
            def _(vector):
                vector.tensor_tensor(s_t0B.ap(), s_Ec.ap(),
                                     s_Ein.ap().unsqueeze(2).broadcast_to([128, 8, 128]),
                                     op=ALU.add)
                vector.drain()
                vector.reciprocal(s_rEgB.ap().rearrange("p h c -> p (h c)"),
                                  s_t0B.ap().rearrange("p h c -> p (h c)"))
                vector.drain()
                vector.tensor_copy(s_rEp2[:], s_Eprev[:])
                vector.drain()
                vector.tensor_copy(s_Eprev.ap().unsqueeze(2), s_t0B[:, :, 127:128])
                vector.drain()
                vector.tensor_tensor(s_t0B.ap(), s_qc.ap(),
                                     s_rEp2.ap().unsqueeze(2).broadcast_to([128, 8, 128]),
                                     op=ALU.mult)
                vector.drain()
                vector.tensor_tensor(s_q2B.ap(), s_t0B.ap(), s_rEgB.ap(), op=ALU.mult)
                vector.reciprocal(s_rEp[:], s_rEp2[:])
                vector.drain()
                vector.tensor_tensor(s_k2B.ap(), s_wc.ap(),
                                     s_rEp.ap().unsqueeze(2).broadcast_to([128, 8, 128]),
                                     op=ALU.mult)
                vector.tensor_tensor(s_kTfB.ap(), s_wc.ap(),
                                     s_rEgB[:, :, 127:128].broadcast_to([128, 8, 128]),
                                     op=ALU.mult)
                vector.tensor_tensor(s_dec.ap().unsqueeze(2), s_rEp2.ap().unsqueeze(2),
                                     s_rEgB[:, :, 127:128], op=ALU.mult)
                vector.drain()
        with nc.Block() as k:
            @k.tensor
            def _(tensor):
                for h in range(8):
                    tensor.transpose(ps_T[:, h * 128:(h + 1) * 128], s_kTfB[:, h, :], s_idbf[:])
                for h in range(8):
                    tensor.matmul(ps_o[:, h * 128:(h + 1) * 128], s_q2B[:, h, :],
                                  s_Sbf[:, h, :], start=(h % 4 == 0), stop=False,
                                  skip_group_check=True)
                for h in range(8):
                    tensor.matmul(ps_AY[:, h * 128:(h + 1) * 128], s_k2B[:, h, :],
                                  s_q2B[:, h, :], start=(h % 4 == 0), stop=True,
                                  skip_group_check=True)
        with nc.Block() as k:
            @k.vector
            def _(vector):
                vector.tensor_copy(s_kTtB.ap().rearrange("p h c -> p (h c)"), ps_T[:])
                vector.tensor_tensor(s_AbB.ap(), ps_AY.ap().rearrange("p (h c) -> p h c", h=8),
                                     s_ut.ap().unsqueeze(1).broadcast_to([128, 8, 128]),
                                     op=ALU.mult)
        with nc.Block() as k:
            @k.tensor
            def _(tensor):
                for h in range(8):
                    tensor.matmul(ps_o[:, h * 128:(h + 1) * 128], s_AbB[:, h, :],
                                  s_vc[:, h * 128:(h + 1) * 128], start=False, stop=True,
                                  skip_group_check=True)
                for h in range(8):
                    tensor.matmul(ps_S[:, h * 128:(h + 1) * 128], s_kTtB[:, h, :],
                                  s_vc[:, h * 128:(h + 1) * 128], start=(h % 4 == 0),
                                  stop=True, skip_group_check=True)
        with nc.Block() as k:
            @k.vector
            def _(vector):
                vector.tensor_tensor(s_Sf.ap(), s_Sf.ap(),
                                     s_dec.ap().unsqueeze(2).broadcast_to([128, 8, 128]),
                                     op=ALU.mult)
                vector.drain()
                vector.tensor_tensor(s_Sf.ap().rearrange("p h c -> p (h c)"),
                                     s_Sf.ap().rearrange("p h c -> p (h c)"),
                                     ps_S[:], op=ALU.add)
                vector.drain()
                vector.tensor_copy(s_Sbf[:], s_Sf[:])
                vector.drain()
        with nc.Block() as k:
            @k.scalar
            def _(scalar):
                scalar.copy(s_o[:, 0:512], ps_o[:, 0:512])
                scalar.copy(s_o[:, 512:1024], ps_o[:, 512:1024])
        with nc.Block() as k:
            sc0, vv0 = sc.n, vv.n

            @k.scalar
            def _(scalar, sc0=sc0, vv0=vv0):
                scalar.square(s_osq[:], s_o[:]).then_inc(sc.h, 1); sc.inc()
                scalar.wait_ge(vv.h, vv0 + 1)
                scalar.activation(s_ost[:, 2:3], s_ost[:, 1:2], AF.Sqrt).then_inc(sc.h, 1); sc.inc()

            @k.vector
            def _(vector, sc0=sc0):
                vector.wait_ge(sc.h, sc0 + 1)
                vector.tensor_reduce(s_ost[:, 0:1], s_osq[:], axis=AX.X, op=ALU.add)
                vector.drain()
                vector.tensor_scalar(s_ost[:, 1:2], s_ost[:, 0:1], 1.0 / D, None, op0=ALU.mult)
                vector.drain()
                vector.tensor_scalar(s_ost[:, 1:2], s_ost[:, 1:2], RMS_EPS, None,
                                     op0=ALU.add).then_inc(vv.h, 1); vv.inc()
                vector.wait_ge(sc.h, sc0 + 2)
                vector.reciprocal(s_ost[:, 3:4], s_ost[:, 2:3])
                vector.drain()
                vector.tensor_copy(s_gc32[:], s_gc[:])
                vector.drain()
                vector.tensor_scalar(s_o[:], s_o[:], s_ost[:, 3:4], None, op0=ALU.mult)
                vector.drain()
                vector.tensor_tensor(s_o[:], s_o[:], s_gn[:], op=ALU.mult)
                vector.drain()
                vector.tensor_tensor(s_rgb[:], s_o[:], s_gc32[:], op=ALU.mult)
        with nc.Block() as k:
            @k.tensor
            def _(tensor):
                for kc in range(8):
                    tensor.transpose(ps_T[:, kc * 128:(kc + 1) * 128],
                                     s_rgb[:, kc * 128:(kc + 1) * 128], s_idbf[:])
        with nc.Block() as k:
            @k.vector
            def _(vector):
                vector.tensor_copy(s_rgT.ap().rearrange("p c t -> p (c t)"), ps_T[:])
        with nc.Block() as k:
            @k.tensor
            def _(tensor):
                for jh in range(2):
                    for kc in range(8):
                        tensor.matmul(ps_AY[:, jh * 512:(jh + 1) * 512], s_rgT[:, kc, :],
                                      s_wo1T[:, kc, jh * 512:(jh + 1) * 512],
                                      start=(kc == 0), stop=(kc == 7),
                                      skip_group_check=True)
        with nc.Block() as k:
            @k.vector
            def _(vector, i=i):
                vector.wait_ge(sg.h, sg.n)
                vector.tensor_copy(s_yat[:, 0:512], ps_AY[:, 0:512])
                vector.tensor_copy(s_yat[:, 512:1024],
                                   ps_AY[:, 512:1024]).then_inc(vv.h, 1)
            vv.inc()

            @k.gpsimd
            def _(gp, i=i, r=r):
                gp.wait_ge(vv.h, vv.n)
                gp.dma_start(ya[r:r + 128, :], s_yat[:]).then_inc(sg.h, 16); sg.inc(16)
    with nc.Block() as k:
        @k.gpsimd
        def _(gp):
            gp.wait_ge(sg.h, sg.n)
    return b


def build_launch4():
    b = B()
    nc = b.nc
    NGR = T // 512
    x1 = b.din("x1", [T, D], F32).ap()
    ya = b.din("ya", [T, D], F32).ap()
    xn1_bf = b.din("xn1_bf", [T, D], BF16).ap()
    wr1 = b.din("wr1", [D, 8], F32).ap()
    br1_row = b.din("br1_row", [1, 8], F32).ap()
    w1 = b.din("w1", [8, D, 4096], F32).ap()
    b1_ = b.din("b1_", [8, 4096], F32).ap()
    w2 = b.din("w2", [8, 4096, D], F32).ap()
    b2_ = b.din("b2_", [8, D], F32).ap()
    ones_row = b.din("ones_row", [1, 128], F32).ap()

    out = b.dout("out", [T, D], F32).ap()
    acc = b.dtmp("acc4", [T, D], F32).ap()

    s_wr1 = b.sb("s_wr1", [128, 8, 8], BF16)
    s_br1 = b.sb("s_br1", [1, 8], BF16)
    s_ones = b.sb("s4_ones", [1, 128], BF16)
    s_w1 = b.sb("s_w1", [128, 8, 4096], BF16)
    s_w2 = b.sb("s_w2", [128, 32, D], BF16)
    s_b1c = b.sb("s_b1c", [128, 32], F32)
    s_b2r = b.sb("s_b2r", [1, D], BF16)
    s_hfm = b.sb("s_hfm", [128, 32, 512], BF16)
    s_xnfm = b.sb("s4_xnfm", [128, 8, 512], BF16)
    s_wcolB = b.sb("s_wcolB", [128, NT, 8], F32)
    s_yw = [b.sb("s_yw0", [128, D], F32), b.sb("s_yw1", [128, D], F32)]
    s_ot = [b.sb("s_ot0", [128, D], F32), b.sb("s_ot1", [128, D], F32)]
    s_xt = b.sb("s4_xt", [128, D], F32)
    s_yat = b.sb("s4_yat", [128, D], F32)
    s_base = b.sb("s4_base", [128, D], F32)
    s_lg = b.sb("s4_lg", [128, 8], F32)
    s_m8 = b.sb("s4_m8", [128, 8], F32)
    s_m1n = b.sb("s4_m1n", [128, 1], F32)
    s_er = b.sb("s4_er", [128, 8], F32)
    s_sum = b.sb("s4_sum", [128, 1], F32)
    s_rs = b.sb("s4_rs", [128, 1], F32)
    s_msk = b.sb("s4_msk", [128, 8], F32)

    ps_a = b.ps("ps4_a", [128, 512])
    ps_b = b.ps("ps4_b", [128, 512])
    ps_y = [b.ps("ps4_y0", [128, 512]), b.ps("ps4_y1", [128, 512])]
    ps_r = b.ps("ps4_r", [128, 8])

    ld = b.sem("ld")
    lg = b.sem("lg")
    lw = b.sem("lw")
    sg = b.sem("sg")
    tv = b.sem("tv")
    vv = b.sem("vv")
    sc = b.sem("sc")

    with nc.Block() as k:
        @k.gpsimd
        def _(g):
            g.dma_start(s_wr1[:], wr1.rearrange("(c p) e -> p c e", p=128)).then_inc(lg.h, 16); lg.inc(16)
            g.dma_start(s_br1[:], br1_row).then_inc(lg.h, 16); lg.inc(16)
            g.dma_start(s_ones[:], ones_row).then_inc(lg.h, 16); lg.inc(16)
            g.wait_ge(lg.h, lg.n)

    for g in range(NGR):
        r0 = g * 512
        with nc.Block() as k:
            @k.sync
            def _(sync, r0=r0):
                for kc in range(8):
                    sync.dma_start_transpose(
                        s_xnfm[:, kc, :], xn1_bf[r0:r0 + 512, kc * 128:(kc + 1) * 128]
                    ).then_inc(ld.h, 16); ld.inc(16)
                sync.wait_ge(ld.h, ld.n)
        for t in range(4):
            i = g * 4 + t
            r = i * 128
            with nc.Block() as k:
                @k.sync
                def _(sync, r=r):
                    sync.dma_start(s_xt[:], x1[r:r + 128, :]).then_inc(ld.h, 16); ld.inc(16)
                    sync.dma_start(s_yat[:], ya[r:r + 128, :]).then_inc(ld.h, 16); ld.inc(16)
                    sync.wait_ge(ld.h, ld.n)

                @k.tensor
                def _(tensor, t=t):
                    for kc in range(8):
                        tensor.matmul(ps_r[:], s_xnfm[:, kc, t * 128:(t + 1) * 128],
                                      s_wr1[:, kc, :], start=(kc == 0), stop=False,
                                      skip_group_check=True)
                    tensor.matmul(ps_r[:], s_ones[:, 0:128], s_br1[:],
                                  start=False, stop=True,
                                  skip_group_check=True).then_inc(tv.h, 1); tv.inc()
            with nc.Block() as k:
                @k.vector
                def _(vector, i=i):
                    vector.tensor_copy(s_lg[:], ps_r[:])
                    vector.drain()
                    vector.max(out=s_m8[:], in_=s_lg[:])
                    vector.drain()
                    vector.tensor_scalar(s_m1n[:], s_m8[:, 0:1], -1.0, None, op0=ALU.mult)
                    vector.tensor_tensor(s_msk[:], s_lg[:],
                                         s_m8[:, 1:2].broadcast_to([128, 8]),
                                         op=ALU.is_ge).then_inc(vv.h, 1); vv.inc()

                @k.scalar
                def _(scalar):
                    scalar.wait_ge(vv.h, vv.n)
                    scalar.activation(s_er[:], s_lg[:], AF.Exp, bias=s_m1n[:]).then_inc(sc.h, 1); sc.inc()

                @k.vector
                def _(vector, i=i):
                    vector.wait_ge(sc.h, sc.n)
                    vector.tensor_tensor(s_er[:], s_er[:], s_msk[:], op=ALU.mult)
                    vector.drain()
                    vector.tensor_reduce(s_sum[:], s_er[:], axis=AX.X, op=ALU.add)
                    vector.drain()
                    vector.reciprocal(s_rs[:], s_sum[:])
                    vector.drain()
                    vector.tensor_scalar(s_wcolB[:, i, :], s_er[:], s_rs[:], None,
                                         op0=ALU.mult)
                    vector.tensor_tensor(s_base[:], s_xt[:], s_yat[:],
                                         op=ALU.add).then_inc(vv.h, 1); vv.inc()

                @k.gpsimd
                def _(gp, r=r):
                    gp.wait_ge(vv.h, vv.n)
                    gp.dma_start(acc[r:r + 128, :], s_base[:]).then_inc(sg.h, 16); sg.inc(16)
            with nc.Block() as k:
                @k.gpsimd
                def _(gp):
                    gp.wait_ge(sg.h, sg.n)

    lw_w1 = [0] * 9
    lw_w2 = [0] * 9
    with nc.Block() as k:
        @k.gpsimd
        def _(gp):
            gp.wait_ge(lg.h, lg.n)
            gp.dma_start(s_w1[:], w1[0].rearrange("(c p) h -> p c h", p=128)).then_inc(lw.h, 16); lw.inc(16)
            gp.dma_start(s_b1c[:], b1_[0].rearrange("(c p) -> p c", p=128), allow_slow_non_contiguous=True).then_inc(lw.h, 16)
            lw_w1[0] = lw.inc(16)
            gp.wait_ge(lw.h, lw.n)
            gp.dma_start(s_w2[:], w2[0].rearrange("(c p) d -> p c d", p=128)).then_inc(lw.h, 16); lw.inc(16)
            gp.dma_start(s_b2r[:], b2_[0].unsqueeze(0)).then_inc(lw.h, 16)
            lw_w2[0] = lw.inc(16)
            gp.wait_ge(lw.h, lw.n)

        @k.sync
        def _(sync):
            for kc in range(8):
                sync.dma_start_transpose(
                    s_xnfm[:, kc, :], xn1_bf[0:512, kc * 128:(kc + 1) * 128]
                ).then_inc(ld.h, 16); ld.inc(16)
            sync.wait_ge(ld.h, ld.n)
    for e in range(8):
        for g in range(NGR):
            r0 = g * 512
            with nc.Block() as k:
                tv0, sc0 = tv.n, sc.n
                ld_full = ld.n

                @k.gpsimd
                def _(gp, e=e, g=g):
                    if g == 0 and e >= 1:
                        gp.wait_ge(lw.h, lw_w1[e])
                        gp.dma_start(s_w2[:], w2[e].rearrange("(c p) d -> p c d", p=128)).then_inc(lw.h, 16); lw.inc(16)
                        gp.dma_start(s_b2r[:], b2_[e].unsqueeze(0)).then_inc(lw.h, 16)
                        lw_w2[e] = lw.inc(16)

                @k.tensor
                def _(tensor, e=e, tv0=tv0, sc0=sc0, ld_full=ld_full):
                    tensor.wait_ge(ld.h, ld_full)
                    tensor.wait_ge(lw.h, lw_w1[e])
                    for hc in range(32):
                        pd = ps_a if hc % 2 == 0 else ps_b
                        if hc >= 2:
                            tensor.wait_ge(sc.h, sc0 + hc - 1)
                        for kc in range(8):
                            mm = tensor.matmul(pd[:], s_w1[:, kc, hc * 128:(hc + 1) * 128],
                                               s_xnfm[:, kc, :], start=(kc == 0),
                                               stop=(kc == 7), skip_group_check=True)
                        mm.then_inc(tv.h, 1); tv.inc()

                @k.scalar
                def _(scalar, e=e, tv0=tv0, ld_full=ld_full):
                    scalar.wait_ge(ld.h, ld_full)
                    scalar.wait_ge(lw.h, lw_w1[e])
                    for hc in range(32):
                        pd = ps_a if hc % 2 == 0 else ps_b
                        scalar.wait_ge(tv.h, tv0 + hc + 1)
                        scalar.activation(s_hfm[:, hc, :], pd[:], AF.Relu,
                                          bias=s_b1c[:, hc:hc + 1]).then_inc(sc.h, 1); sc.inc()
            with nc.Block() as k:
                tb, vb, sgb, ldb = tv.n, vv.n, sg.n, ld.n

                @k.sync
                def _(sync, e=e, g=g, vb=vb, sgb=sgb, ldb=ldb):
                    sync.wait_ge(sg.h, sgb)
                    for t in range(4):
                        r = g * 512 + t * 128
                        if t >= 2:
                            sync.wait_ge(vv.h, vb + 2 * t - 2)
                        sync.dma_start(s_ot[t % 2][:],
                                       acc[r:r + 128, :]).then_inc(ld.h, 16)
                        ld.inc(16)
                        sync.wait_ge(ld.h, ld.n)
                    if not (e == 7 and g == NGR - 1):
                        nr0 = ((g + 1) % NGR) * 512
                        for kc in range(8):
                            sync.dma_start_transpose(
                                s_xnfm[:, kc, :],
                                xn1_bf[nr0:nr0 + 512, kc * 128:(kc + 1) * 128]
                            ).then_inc(ld.h, 16); ld.inc(16)

                @k.tensor
                def _(tensor, e=e, tb=tb, vb=vb):
                    tensor.wait_ge(lw.h, lw_w2[e])
                    for p in range(8):
                        t, jh = p // 2, p % 2
                        if p >= 2:
                            tensor.wait_ge(vv.h, vb + p - 1)
                        for hc in range(32):
                            tensor.matmul(ps_y[p % 2][:], s_hfm[:, hc, t * 128:(t + 1) * 128],
                                          s_w2[:, hc, jh * 512:(jh + 1) * 512],
                                          start=(hc == 0), stop=False, skip_group_check=True)
                        tensor.matmul(ps_y[p % 2][:], s_ones[:, 0:128],
                                      s_b2r[:, jh * 512:(jh + 1) * 512],
                                      start=False, stop=True,
                                      skip_group_check=True).then_inc(tv.h, 1); tv.inc()

                @k.vector
                def _(vector, e=e, g=g, tb=tb, sgb=sgb, ldb=ldb, vb=vb):
                    for p in range(8):
                        t, jh = p // 2, p % 2
                        i = g * 4 + t
                        vector.wait_ge(tv.h, tb + p + 1)
                        vector.wait_ge(ld.h, ldb + (t + 1) * 16)
                        if t >= 2 and jh == 0:
                            vector.wait_ge(sg.h, sgb + (t - 1) * 16)
                        vector.scalar_tensor_tensor(
                            s_yw[t % 2][:, jh * 512:(jh + 1) * 512], ps_y[p % 2][:],
                            s_wcolB[:, i, e:e + 1],
                            s_ot[t % 2][:, jh * 512:(jh + 1) * 512],
                            op0=ALU.mult, op1=ALU.add).then_inc(vv.h, 1); vv.inc()
                        vector.drain()

                @k.gpsimd
                def _(gp, e=e, g=g, vb=vb):
                    dst = out if e == 7 else acc
                    for t in range(4):
                        r = g * 512 + t * 128
                        gp.wait_ge(vv.h, vb + 2 * t + 2)
                        gp.dma_start(dst[r:r + 128, :],
                                     s_yw[t % 2][:]).then_inc(sg.h, 16); sg.inc(16)
                        gp.wait_ge(sg.h, sg.n)
                    if g == NGR - 1 and e < 7:
                        gp.wait_ge(lw.h, lw.n)
                        gp.dma_start(s_w1[:], w1[e + 1].rearrange("(c p) h -> p c h", p=128)).then_inc(lw.h, 16); lw.inc(16)
                        gp.dma_start(s_b1c[:], b1_[e + 1].rearrange("(c p) -> p c", p=128), allow_slow_non_contiguous=True).then_inc(lw.h, 16)
                        lw_w1[e + 1] = lw.inc(16)
    with nc.Block() as k:
        @k.gpsimd
        def _(gp):
            gp.wait_ge(sg.h, sg.n)
    return b


def launch4_inputs(inp, r2, r3b):
    f32 = np.float32
    shared = dict(
        wr1=np.ascontiguousarray(np.asarray(inp["L1_ffn_Wr"])),
        br1_row=np.ascontiguousarray(np.asarray(inp["L1_ffn_br"]).reshape(1, 8)),
        w1=np.ascontiguousarray(np.asarray(inp["L1_ffn_W1"])),
        b1_=np.ascontiguousarray(np.asarray(inp["L1_ffn_b1"])),
        w2=np.ascontiguousarray(np.asarray(inp["L1_ffn_W2"])),
        b2_=np.ascontiguousarray(np.asarray(inp["L1_ffn_b2"])),
        ones_row=np.ones((1, 128), f32),
    )
    return [dict(shared, x1=r2[c]["x1"], ya=r3b[c]["ya"], xn1_bf=r2[c]["xn1_bf"])
            for c in range(NCORE)]



def _shard_tokens(arr):
    out = []
    for c in range(NCORE):
        bb, s = c // 4, c % 4
        out.append(np.ascontiguousarray(arr[bb, s * T:(s + 1) * T]))
    return out


def launch1_inputs(inp):
    f32 = np.float32
    wq = np.ascontiguousarray(inp["L0_attn_Wq"]).astype(f32, copy=False)
    wk = np.ascontiguousarray(inp["L0_attn_Wk"]).astype(f32, copy=False)
    shared = dict(
        wq_h=np.ascontiguousarray(wq.reshape(16, 64, D)),
        wk_h=np.ascontiguousarray(wk.reshape(16, 64, D)),
        w0=np.ascontiguousarray(inp["L0_attn_Wrf"][0]),
        bq2=np.ascontiguousarray(inp["L0_attn_bq"].reshape(16, 64).T),
        bk2=np.ascontiguousarray(inp["L0_attn_bk"].reshape(16, 64).T),
        wvT=np.ascontiguousarray(inp["L0_attn_Wv"].T),
        wkT=np.ascontiguousarray(wk.T),
        bv_row=np.ascontiguousarray(inp["L0_attn_bv"].reshape(1, D)),
        bk_row=np.ascontiguousarray(inp["L0_attn_bk"].reshape(1, D)),
        ones_row=np.ones((1, 128), f32),
        ones_col=np.ones((128, 1), f32),
    )
    xs = _shard_tokens(np.asarray(inp["L0_attn_x"]))
    return [dict(shared, x=xs[c]) for c in range(NCORE)]


def launch2_inputs(inp, r1):
    f32 = np.float32
    wr = np.asarray(inp["L0_ffn_Wr"])
    ws = np.asarray(inp["L0_ffn_Ws"])
    selp = np.zeros((8, 16, 128), f32)
    for p in range(8):
        selp[p, 2 * p, 0:64] = 1.0
        selp[p, 2 * p + 1, 64:128] = 1.0
    shared = dict(
        woT=np.ascontiguousarray(np.asarray(inp["L0_attn_Wo"]).T),
        bo_row=np.ascontiguousarray(np.asarray(inp["L0_attn_bo"]).reshape(1, D)),
        centT=np.ascontiguousarray(np.asarray(inp["L0_ffn_centroids"]).T),
        eb_row=np.ascontiguousarray(np.asarray(inp["L0_ffn_expert_biases"]).reshape(1, 16)),
        wrT=np.ascontiguousarray(wr.transpose(0, 2, 1)),
        br_e=np.ascontiguousarray(np.asarray(inp["L0_ffn_br"])),
        wsT=np.ascontiguousarray(ws.transpose(0, 2, 1)),
        bs2=np.ascontiguousarray(np.asarray(inp["L0_ffn_bs"]).reshape(1, 2 * D)),
        selp=selp,
        ones_row=np.ones((1, 128), f32),
        ones_col=np.ones((128, 1), f32),
        ident=np.eye(128, dtype=f32),
    )
    xs = _shard_tokens(np.asarray(inp["L0_attn_x"]))
    out = []
    for c in range(NCORE):
        bb = c // 4
        grp = range(bb * 4, bb * 4 + 4)
        out.append(dict(
            shared, x=xs[c],
            xn_bf=r1[c]["xn_bf"], phi_bf=r1[c]["phi_bf"],
            statsN4=np.stack([r1[cc]["stats_N"] for cc in grp]),
            statsD4=np.stack([r1[cc]["stats_D"] for cc in grp]),
        ))
    return out


def launch3a_inputs(inp, r2):
    f32 = np.float32
    shared = dict(
        wq1T=np.ascontiguousarray(np.asarray(inp["L1_attn_Wq"]).T),
        wk1T=np.ascontiguousarray(np.asarray(inp["L1_attn_Wk"]).T),
        wv1T=np.ascontiguousarray(np.asarray(inp["L1_attn_Wv"]).T),
        wg1T=np.ascontiguousarray(np.asarray(inp["L1_attn_Wg1"]).T),
        wg2T=np.ascontiguousarray(np.asarray(inp["L1_attn_Wg2"]).T),
        identf=np.eye(128, dtype=f32),
    )
    return [dict(shared, xn1_bf=r2[c]["xn1_bf"]) for c in range(NCORE)]


def launch3b_inputs(inp, r3a):
    f32 = np.float32
    ut = np.triu(np.ones((128, 128), f32))
    shared = dict(
        wo1T=np.ascontiguousarray(np.asarray(inp["L1_attn_Wo"]).T),
        gn_row=np.ascontiguousarray(np.asarray(inp["L1_attn_g_norm_w"]).reshape(1, D)),
        identf=np.eye(128, dtype=f32),
        ut=ut,
    )
    out = []
    for c in range(NCORE):
        bb, seg = c // 4, c % 4
        grp = range(bb * 4, bb * 4 + 4)
        mask4 = np.zeros((1, 4), f32)
        mask4[0, :seg] = 1.0
        out.append(dict(
            shared,
            q_fm=r3a[c]["q_fm"], w_fm=r3a[c]["w_fm"], E_fm=r3a[c]["E_fm"],
            v_tm=r3a[c]["v_tm"], gate_tm=r3a[c]["gate_tm"],
            Wc4=np.stack([r3a[cc]["Wc"] for cc in grp]),
            Clin4=np.stack([r3a[cc]["Clin"] for cc in grp]),
            mask4=mask4,
            segflag=np.full((1, 1), 1.0 if seg == 0 else 0.0, f32),
        ))
    return out


def kernel(**inp):
    r1 = run_bass_kernel_spmd(build_launch1().nc, launch1_inputs(inp),
                              core_ids=list(range(NCORE))).results
    r2 = run_bass_kernel_spmd(build_launch2().nc, launch2_inputs(inp, r1),
                              core_ids=list(range(NCORE))).results
    r3a = run_bass_kernel_spmd(build_launch3a().nc, launch3a_inputs(inp, r2),
                               core_ids=list(range(NCORE))).results
    r3b = run_bass_kernel_spmd(build_launch3b().nc, launch3b_inputs(inp, r3a),
                               core_ids=list(range(NCORE))).results
    r4 = run_bass_kernel_spmd(build_launch4().nc, launch4_inputs(inp, r2, r3b),
                              core_ids=list(range(NCORE))).results
    out = np.zeros((2, 16384, D), np.float32)
    for c in range(NCORE):
        bb, sseg = c // 4, c % 4
        out[bb, sseg * T:(sseg + 1) * T] = r4[c]["out"]
    return out


if __name__ == "__main__":
    build_launch1()
    print("built ok")
